# version 06
# baseline (speedup 1.0000x reference)
import math
from contextlib import ExitStack

import numpy as np
import concourse.bass as bass
import concourse.mybir as mybir
from concourse.bass_utils import run_bass_kernel_spmd

F32 = mybir.dt.float32
BF16 = mybir.dt.bfloat16
I32 = mybir.dt.int32
U8 = mybir.dt.uint8
AF = mybir.ActivationFunctionType
ALU = mybir.AluOpType
AX = mybir.AxisListType.X

D = 2048
DC = 16
NH = 16
DH = 128
FF = 8192
FFC = 64
HM = 1408
HMC = 11
NE = 8
NEG = -1.0e9
CSHIFT = -20.0
LN2PI = 2.0 * math.pi


class Cfg:
    def __init__(self, ncores=8, S=2048, debug=()):
        self.ncores = ncores
        self.S = S
        self.T = S // 2
        self.SC = min(512, self.T)
        self.TCH = min(512, self.T)
        self.NSC = self.S // self.SC
        self.NTC = self.T // self.TCH
        self.KT = self.S // 128
        self.debug = set(debug)
        self.RING = 8



def _pretile(wT):
    Din, Dout = wT.shape
    ci, co = Din // 128, Dout // 128
    return np.ascontiguousarray(
        wT.reshape(ci, 128, co, 128).transpose(2, 1, 0, 3))


def _deinterleave_cols(wT):
    Din, Dout = wT.shape
    w = wT.reshape(Din, NH, DH)
    perm = np.concatenate([np.arange(0, DH, 2), np.arange(1, DH, 2)])
    return np.ascontiguousarray(w[:, :, perm].reshape(Din, Dout))


def _vec_T(v):
    return np.ascontiguousarray(v.reshape(-1, 128).T)


def host_layout(inputs, cfg):
    f = np.float32
    out = {}
    out["wv0_til"] = _pretile(np.ascontiguousarray(
        inputs["L0_attn_Wqkv"][:, :, 256:384].reshape(D, D)).astype(f, copy=False))
    out["bv0_T"] = np.ascontiguousarray(
        inputs["L0_attn_bqkv"][:, 256:384].astype(f).T)
    out["wout_til"] = _pretile(inputs["L0_attn_Wout"].reshape(D, D).astype(f, copy=False))
    out["bout_T"] = _vec_T(inputs["L0_attn_bout"].astype(f))
    out["sbias_T"] = _vec_T(inputs["L0_ffn_simple_bias"].astype(f))
    out["router_til"] = np.ascontiguousarray(
        inputs["L0_ffn_w_router"].astype(f).T.reshape(DC, 128, 3))
    out["wdown_til"] = np.ascontiguousarray(
        inputs["L0_ffn_w_down"].astype(f).T.reshape(DC, 128, 32))
    out["wup_T"] = np.ascontiguousarray(inputs["L0_ffn_w_up"].astype(f).T)
    for nm in ("ln1_scale", "ln1_bias", "ln2_scale", "ln2_bias"):
        out[nm] = _vec_T(inputs[f"L0_ffn_{nm}"].astype(f))
    for nm in ("wq", "wk", "wv", "wo"):
        out[f"l0{nm}_til"] = _pretile(np.ascontiguousarray(
            inputs[f"L0_ffn_{nm}"].astype(f).T))
    out["ff1_til"] = _pretile(np.ascontiguousarray(inputs["L0_ffn_w_ff1"].astype(f).T))
    out["ff2_til"] = _pretile(np.ascontiguousarray(inputs["L0_ffn_w_ff2"].astype(f).T))
    for nm in ("Wq", "Wk"):
        out[f"l1{nm.lower()}_til"] = _pretile(_deinterleave_cols(
            np.ascontiguousarray(inputs[f"L1_attn_{nm}"].astype(f).T)))
    for nm in ("Wv", "Wo"):
        out[f"l1{nm.lower()}_til"] = _pretile(np.ascontiguousarray(
            inputs[f"L1_attn_{nm}"].astype(f).T))
    out["gate_til"] = np.ascontiguousarray(
        inputs["L1_ffn_gate_w"].astype(f).T.reshape(DC, 128, NE))
    out["gate_b"] = np.ascontiguousarray(inputs["L1_ffn_gate_b"].astype(f).reshape(NE, 1))
    out["bias_e"] = np.ascontiguousarray(inputs["L1_ffn_bias"].astype(f).reshape(NE, 1))
    out["w1_til"] = np.ascontiguousarray(np.stack(
        [_pretile(np.ascontiguousarray(inputs["L1_ffn_w1"][e].astype(f).T))
         for e in range(NE)]))
    out["b1_til"] = np.ascontiguousarray(
        inputs["L1_ffn_b1"].astype(f).reshape(NE, HMC, 128).transpose(0, 2, 1))
    out["w2_til"] = np.ascontiguousarray(np.stack(
        [_pretile(np.ascontiguousarray(inputs["L1_ffn_w2"][e].astype(f).T))
         for e in range(NE)]))
    out["b2"] = np.ascontiguousarray(inputs["L1_ffn_b2"].astype(f))

    import ml_dtypes
    out["ident_bf"] = np.eye(128, dtype=ml_dtypes.bfloat16)
    out["ident_f32"] = np.eye(128, dtype=f)
    SC = cfg.SC
    i = np.arange(128)[:, None]
    j = np.arange(SC)[None, :]
    out["maskband"] = np.ascontiguousarray(np.stack(
        [np.where(i + o <= j, 0.0, NEG).astype(f) for o in range(0, SC, 128)]))
    return out


def shard_x(x, cfg, core):
    b, half = core // 2, core % 2
    return np.ascontiguousarray(x[b].T)



class Sem:
    __slots__ = ("h", "n", "name")

    def __init__(self, h, name):
        self.h = h
        self.n = 0
        self.name = name


class WChunk:
    def __init__(self, idx, slot, cin, null=False, quad=False):
        self.idx = idx
        self.slot = slot
        self.cin = cin
        self.null = null
        self.quad = quad
        self.ap = None
        self.rank = 0


class Builder:
    def __init__(self, cfg, mode="fused", group=None):
        self.cfg = cfg
        self.mode = mode
        self.group = group
        self.nc = bass.Bass("TRN2", target_bir_lowering=False)
        self.es = ExitStack()
        self.phase_idx = 0
        self.barrier_engines = 4
        self._dbg = {}
        self._dump_pending = []
        self._mkio()
        self._mksems()
        self._mksbuf()
        self._wchunks = 0
        self._wloads = 0

    def _mkio(self):
        nc, cfg = self.nc, self.cfg
        S, T = cfg.S, cfg.T

        def di(name, shape, dtype=F32):
            return nc.dram_tensor(name, list(shape), dtype, kind="ExternalInput")

        g = self.group
        self.hin = {}
        names = {
            "xseq": (D, S), "wv0_til": (16, 128, 16, 128), "bv0_T": (128, 16),
            "wout_til": (16, 128, 16, 128), "bout_T": (128, 16),
            "sbias_T": (128, 16), "router_til": (16, 128, 3), "wdown_til": (16, 128, 32),
            "wup_T": (32, 2048),
            "ln1_scale": (128, 16), "ln1_bias": (128, 16),
            "ln2_scale": (128, 16), "ln2_bias": (128, 16),
            "l0wq_til": (16, 128, 16, 128), "l0wk_til": (16, 128, 16, 128),
            "l0wv_til": (16, 128, 16, 128), "l0wo_til": (16, 128, 16, 128),
            "ff1_til": (64, 128, 16, 128), "ff2_til": (16, 128, 64, 128),
            "l1wq_til": (16, 128, 16, 128), "l1wk_til": (16, 128, 16, 128),
            "l1wv_til": (16, 128, 16, 128), "l1wo_til": (16, 128, 16, 128),
            "gate_til": (16, 128, 8), "gate_b": (8, 1), "bias_e": (8, 1),
            "w1_til": (8, 11, 128, 16, 128), "b1_til": (8, 128, 11),
            "w2_til": (8, 16, 128, 11, 128), "b2": (8, 2048),
            "ident_bf": (128, 128), "ident_f32": (128, 128),
            "maskband": (cfg.SC // 128, 128, cfg.SC),
            "pmask": (128, 1), "pos_rot": (1, S),
        }
        g0_only = {"xseq", "wv0_til", "bv0_T", "wout_til", "bout_T", "sbias_T",
                   "router_til", "wdown_til", "wup_T", "ln1_scale", "ln1_bias",
                   "ln2_scale", "ln2_bias", "l0wq_til", "l0wk_til", "l0wv_til",
                   "l0wo_til", "ff1_til", "ff2_til"}
        g1_only = {"l1wq_til", "l1wk_til", "l1wv_til", "l1wo_til", "gate_til",
                   "gate_b", "bias_e", "w1_til", "b1_til", "w2_til", "b2"}
        for nm, sh in names.items():
            if g == 0 and nm in g1_only:
                continue
            if g == 1 and nm in g0_only:
                continue
            dt = BF16 if nm == "ident_bf" else F32
            self.hin[nm] = di(nm, sh, dt).ap()

        self.scr_xn = nc.dram_tensor("scr_xn", [D, S], BF16).ap()
        self.scr_g = nc.dram_tensor("scr_g", [D, T], F32).ap()
        self.scr_y = nc.dram_tensor("scr_y", [D, T], F32).ap()
        self.scr_f = nc.dram_tensor("scr_f", [D, T], F32).ap()
        if self.mode == "fused":
            self.cc_in = nc.dram_tensor("cc_in", [D, T], F32)
            self.cc_out = nc.dram_tensor("cc_out", [D, T], F32)
        elif g == 0:
            self.x1_out = nc.dram_tensor("x1_out", [D, T], F32, kind="ExternalOutput")
            self.cc_in = self.x1_out
        else:
            self.cc_in = nc.dram_tensor("cc_in_x", [D, T], F32, kind="ExternalInput")
            self.cc_out = nc.dram_tensor("cc_sum", [D, T], F32, kind="ExternalInput")
        if self.mode == "fused" or g == 1:
            self.y_ext = nc.dram_tensor("y_ext", [D, T], F32, kind="ExternalOutput").ap()

    def dbg(self, name, shape, dtype=F32):
        t = self.nc.dram_tensor("dbg_" + name, list(shape), dtype, kind="ExternalOutput")
        self._dbg[name] = t
        return t.ap()

    def _mksems(self):
        nc = self.nc
        self.SEM = {}
        for nm in ("BAR", "HW", "HWS", "WL", "WF", "T", "V", "S", "G", "CC", "GD", "HA", "HB"):
            self.SEM[nm] = Sem(self.es.enter_context(nc.semaphore("sem_" + nm)), nm)
        self.WLS = [Sem(self.es.enter_context(nc.semaphore(f"sem_wl{i}")), f"wl{i}")
                    for i in range(self.cfg.RING)]
        self._wl_uses = [0] * self.cfg.RING
        self.chans = {}
        self.dma_sems = [self.SEM["HW"], self.SEM["HWS"], self.SEM["HA"], self.SEM["HB"]]

    def ch(self, name):
        if name not in self.chans:
            s = Sem(self.es.enter_context(self.nc.semaphore("ch_" + name)), name)
            self.chans[name] = s
            self.dma_sems.append(s)
        return self.chans[name]

    def inc(self, instr, sem, k=1):
        sem.n += k
        instr.then_inc(sem.h, k)
        return sem.n

    def wait(self, eng, sem, val=None):
        eng.wait_ge(sem.h, sem.n if val is None else val)

    def _mksbuf(self):
        nc, cfg = self.nc, self.cfg
        A = nc.alloc_sbuf_tensor
        S, T = cfg.S, cfg.T
        self.wring = A("wring", [128, cfg.RING, 16, 128], BF16)
        self.ctx_all = A("ctx_all", [128, DC, T], BF16)
        self.consts = {}
        for nm, sh, dt in (
            ("ident_bf", [128, 128], BF16), ("ident_f32", [128, 128], F32),
            ("maskband", [128, cfg.SC // 128, cfg.SC], F32),
            ("bv0_T", [128, 16], F32), ("bout_T", [128, 16], F32),
            ("sbias_T", [128, 16], F32),
            ("ln1_scale", [128, 16], F32), ("ln1_bias", [128, 16], F32),
            ("ln2_scale", [128, 16], F32), ("ln2_bias", [128, 16], F32),
            ("router_til", [128, 16, 3], F32), ("wdown_til", [128, 16, 32], BF16),
            ("wup_T", [32, 2048], BF16), ("gate_til", [128, 16, 8], F32),
            ("gate_b", [8, 1], F32), ("bias_e", [8, 1], F32),
            ("b1_til", [128, 8, 11], F32), ("b2", [8, 2048], BF16),
            ("ones_col_bf", [128, 1], BF16), ("ones_row_f", [1, 128], F32),
            ("ones_row_bf", [1, 128], BF16),
            ("bvsum", [128, 1], F32), ("gbsum", [8, 1], F32),
            ("pmask", [128, 1], F32), ("pm_c", [128, 1], F32),
            ("cshift", [128, 1], F32),
        ):
            self.consts[nm] = A("c_" + nm, sh, dt)
        self.lg_fm = A("lg_fm", [8, T], F32)
        self.comb_fm = A("comb_fm", [8, T], BF16)
        self.sel2f = A("sel2f", [128, T], U8)
        self.ARN = (nc.sbuf_bytes_remaining - 512) // 32 * 32
        self.AR = A("arena", [128, self.ARN], U8)
        self.ps = [nc.alloc_psum_tensor(f"ps{i}", [128, 512], F32) for i in range(8)]
        rem = nc.sbuf_bytes_remaining
        assert rem >= 0, f"SBUF over budget by {-rem}"

    def view(self, off_bytes, shape, dtype, arena=None):
        arena = arena or self.AR
        n = int(np.prod(shape[1:])) * mybir.dt.size(dtype)
        assert off_bytes + n <= self.ARN, (off_bytes, n)
        ap = arena.ap()[0:shape[0], off_bytes:off_bytes + n].bitcast(dtype)
        if len(shape) == 2:
            return ap
        assert len(shape) == 3
        return ap.rearrange("p (a b) -> p a b", b=shape[2])

    def wplan(self, aps, cin=16, quad=False):
        out = []
        if quad:
            assert len(aps) == 4 and self.cfg.RING % 4 == 0
            while self._wchunks % 4 != 0:
                h = WChunk(self._wchunks, self._wchunks % self.cfg.RING, 0, null=True)
                h.ap = None
                h.rank = self._wloads
                self._wchunks += 1
                out.append(h)
        for ap in aps:
            h = WChunk(self._wchunks, self._wchunks % self.cfg.RING, cin, quad=quad)
            h.ap = ap
            self._wl_uses[h.slot] += 1
            h.rank = self._wl_uses[h.slot]
            self._wchunks += 1
            out.append(h)
        return out

    def emit_wloads(self, g, handles):
        WL, WF = self.SEM["WL"], self.SEM["WF"]
        for h in handles:
            if h.idx >= self.cfg.RING:
                g.wait_ge(WF.h, h.idx - self.cfg.RING + 1)
            if h.null:
                continue
            ins = g.dma_start(out=self.wring[:, h.slot, 0:h.cin, :], in_=h.ap)
            WL.n += 16
            ins.then_inc(WL.h, 16)

    def wwait(self, te, h):
        if not h.null:
            te.wait_ge(self.SEM["WL"].h, 16 * h.rank)

    def wview(self, h, k=None, quad=False):
        if quad:
            return self.wring[:, h.slot:h.slot + 4, k, :]
        if k is None:
            return self.wring[:, h.slot, 0:h.cin, :]
        return self.wring[:, h.slot, k, :]

    def wrel(self, instr, hs):
        if not isinstance(hs, (list, tuple)):
            hs = [hs]
        self.inc(instr, self.SEM["WF"], len(hs))

    def sched(self):
        return Sched(self)

    def run_phase(self, name, sc, dumps=(), final=False):
        b = self
        bar = b.SEM["BAR"]
        k = b.phase_idx
        with b.nc.Block(name) as blk:
            for engname in ("tensor", "vector", "scalar", "sync"):
                ops = sc.q[engname]

                def wrapped(eng, ops=ops, engname=engname):
                    if k > 0:
                        eng.wait_ge(bar.h, 4 * k)
                    if engname == "sync" and final:
                        hw = b.SEM["HW"]
                        if b._dump_pending:
                            eng.wait_ge(hw.h, hw.n)
                            eng.wait_ge(b.SEM["HWS"].h, b.SEM["HWS"].n)
                        for ap_dst, ap_src in b._dump_pending:
                            ins = eng.dma_start(out=ap_dst, in_=ap_src)
                            hw.n += 16
                            ins.then_inc(hw.h, 16)
                        b._dump_pending = []
                    last = None
                    for op in ops:
                        kind = op[0]
                        if kind == "do":
                            last = op[1](eng)
                        elif kind == "wait":
                            eng.wait_ge(op[1], op[2])
                        elif kind == "inc":
                            assert last is not None, (name, engname, "inc w/o instr")
                            last.then_inc(op[1], op[2])
                        elif kind == "drain":
                            eng.drain()
                    if engname == "sync":
                        for s_ in b.dma_sems:
                            if s_.n:
                                eng.wait_ge(s_.h, s_.n)
                    if final and engname == "sync":
                        eng.wait_ge(b.SEM["GD"].h, b.SEM["GD"].n)
                        for s_ in b.WLS:
                            if s_.n:
                                eng.wait_ge(s_.h, s_.n)
                    ins = eng.drain()
                    ins.then_inc(bar.h, 1)
                getattr(blk, engname)(wrapped)
                bar.n += 1
            if sc.q["gpsimd"]:
                def gwrapped(g, ops=sc.q["gpsimd"]):
                    last = None
                    for op in ops:
                        kind = op[0]
                        if kind == "do":
                            last = op[1](g)
                        elif kind == "wait":
                            g.wait_ge(op[1], op[2])
                        elif kind == "inc":
                            last.then_inc(op[1], op[2])
                blk.gpsimd(gwrapped)
        b.phase_idx += 1
        for nm, ap in dumps:
            if name in b.cfg.debug or "*" in b.cfg.debug:
                dst = b.dbg(f"{name}_{nm}", list(ap.shape),
                            ap.dtype)
                b._dump_pending.append((dst, ap))


class Sched:
    def __init__(self, b):
        self.b = b
        self.q = {"tensor": [], "vector": [], "scalar": [], "sync": [], "gpsimd": []}

    def do(self, eng, fn):
        self.q[eng].append(("do", fn))

    def wait(self, eng, sem, val=None):
        self.q[eng].append(("wait", sem.h, sem.n if val is None else val))

    def inc(self, eng, sem, k=1):
        sem.n += k
        if self.q[eng] and self.q[eng][-1][0] != "do":
            self.q[eng].append(("do", lambda e: e.drain()))
        self.q[eng].append(("inc", sem.h, k))
        return sem.n

    def drain(self, eng):
        self.q[eng].append(("drain",))

    def dma(self, eng, out, in_, sem=None):
        self.do(eng, lambda e, out=out, in_=in_: e.dma_start(out=out, in_=in_))
        sem = sem or self.b.SEM["HW"]
        return self.inc(eng, sem, 16)

    def wloads(self, handles):
        b = self.b
        WF = b.SEM["WF"]
        for h in handles:
            if h.idx >= b.cfg.RING:
                self.q["gpsimd"].append(("wait", WF.h, h.idx - b.cfg.RING + 1))
            if h.null:
                continue
            self.do("gpsimd", lambda g, h=h: g.dma_start(
                out=b.wring[:, h.slot, 0:h.cin, :], in_=h.ap))
            self.inc("gpsimd", b.WLS[h.slot], 16)

    def wwait(self, h):
        if not h.null:
            self.q["tensor"].append(("wait", self.b.WLS[h.slot].h, 16 * h.rank))



def p_consts(b):
    sc = b.sched()
    C = b.consts
    HW, SV = b.SEM["HW"], b.SEM["V"]
    for nm in ("ident_bf", "ident_f32", "maskband", "bv0_T", "bout_T",
               "sbias_T", "ln1_scale", "ln1_bias", "ln2_scale", "ln2_bias",
               "router_til", "wdown_til", "wup_T", "gate_til", "gate_b",
               "bias_e", "b1_til", "b2", "pmask"):
        if nm not in b.hin:
            continue
        src = b.hin[nm]
        if nm == "maskband":
            src = src.rearrange("a p s -> p a s")
        if nm == "b1_til":
            src = src.rearrange("e p h -> p e h")
        if nm in ("router_til", "wdown_til", "gate_til"):
            src = src.rearrange("k p r -> p k r")
        if nm in ("wup_T", "b2", "wdown_til"):
            sc.dma("gpsimd", C[nm].ap(), src, sem=b.SEM["GD"])
        else:
            sc.dma("sync", C[nm].ap(), src)
    sc.wait("vector", HW)
    sc.wait("vector", b.SEM["GD"])
    sc.do("vector", lambda v: v.memset(C["ones_col_bf"].ap(), 1.0))
    sc.do("vector", lambda v: v.memset(C["ones_row_f"].ap(), 1.0))
    sc.do("vector", lambda v: v.memset(C["ones_row_bf"].ap(), 1.0))
    sc.do("vector", lambda v: v.memset(C["cshift"].ap(), CSHIFT))
    if "bv0_T" in b.hin:
        sc.do("vector", lambda v: v.reduce_sum(C["bvsum"].ap(), C["bv0_T"].ap(), axis=AX))
    if "gate_b" in b.hin:
        sc.do("vector", lambda v: v.tensor_add(C["gbsum"].ap(), C["gate_b"].ap(),
                                               C["bias_e"].ap()))
    if "pmask" in b.hin:
        sc.do("vector", lambda v: v.tensor_scalar_add(C["pm_c"].ap(), C["pmask"].ap(),
                                                      CSHIFT))
    sc.inc("vector", SV)
    b.run_phase("consts", sc)


def p_ln_outer(b, block):
    cfg = b.cfg
    S, T = cfg.S, cfg.T
    SC = cfg.SC
    NSC = S // SC
    NOC = T // SC
    name = f"b{block}_ln"
    V = b.view
    o = 0
    xin = V(o, [128, DC, SC], F32); o += DC * SC * 4
    ob = None
    if block == 1:
        ob = [V(o, [128, SC], F32), V(o + SC * 4, [128, SC], F32)]
        o += 2 * SC * 4
    xbf = V(o, [128, DC, SC], BF16); o += DC * SC * 2
    sqbf = V(o, [128, DC, SC], BF16); o += DC * SC * 2
    xnbf = V(o, [128, DC, SC], BF16); o += DC * SC * 2
    xnA = [V(o, [128, SC], F32), V(o + SC * 4, [128, SC], F32)]; o += 2 * SC * 4
    xnB = [V(o, [128, SC], F32), V(o + SC * 4, [128, SC], F32)]; o += 2 * SC * 4
    mu = V(o, [1, SC], F32); o += SC * 4
    musq = V(o, [1, SC], F32); o += SC * 4
    var = V(o, [1, SC], F32); o += SC * 4
    lnv = V(o, [1, SC], F32); o += SC * 4
    rstd = V(o, [1, SC], F32); o += SC * 4
    ps_sum, ps_sq, ps_mu, ps_rstd, ps_lg = (p.ap()[:, 0:SC] for p in b.ps[:5])
    nlg = 3 if block == 0 else 8
    lg_til = b.consts["router_til" if block == 0 else "gate_til"].ap()
    HW, ST, SV, SS = b.SEM["HW"], b.SEM["T"], b.SEM["V"], b.SEM["S"]
    sc = b.sched()
    for scn in range(NSC):
        own = scn < NOC
        xsrc = xin if (block == 0 or own or b.mode != "fused") else xbf
        sc.wait("sync", SV)
        sc.wait("sync", SS)
        if "b1dbg" in b.cfg.debug:
            sc.wait("sync", b.ch("dbg1"))
        diffmode = (block == 1 and not own and b.mode == "fused")
        if block == 0:
            src = b.hin["xseq"][:, scn * SC:(scn + 1) * SC].rearrange(
                "(k p) s -> p k s", p=128)
            sc.dma("sync", xin, src)
        else:
            if scn == 0 and b.mode == "fused":
                sc.wait("sync", b.SEM["CC"])
            cols = slice((scn % NOC) * SC, (scn % NOC + 1) * SC)
            if not own:
                srcb = b.cc_out if b.mode == "fused" else b.cc_in
                sc.dma("sync", xin, srcb.ap()[:, cols].rearrange(
                    "(k p) s -> p k s", p=128))
            else:
                sc.dma("sync", xin, b.cc_in.ap()[:, cols].rearrange(
                    "(k p) s -> p k s", p=128))
        ld_hw = HW.n
        if diffmode:
            obfree = [0, 0]
            cols = slice((scn % NOC) * SC, (scn % NOC + 1) * SC)
            for k in range(DC):
                cho = b.ch(f"lnob{k % 2}")
                if obfree[k % 2]:
                    sc.wait("sync", SV, obfree[k % 2])
                sc.dma("sync", ob[k % 2],
                       b.cc_in.ap()[k * 128:(k + 1) * 128, cols], sem=cho)
                if k == 0:
                    sc.wait("vector", HW, ld_hw)
                sc.wait("vector", cho)
                sc.drain("vector")
                sc.do("vector", lambda v, k=k: v.tensor_sub(xbf[:, k, :], xin[:, k, :],
                                                            ob[k % 2]))
                obfree[k % 2] = sc.inc("vector", SV)
            sc.wait("scalar", SV)
            sc.wait("scalar", b.ch(f"ln{block}s"))
            for k in range(DC):
                sc.do("scalar", lambda s, k=k: s.activation(
                    sqbf[:, k, :], xbf[:, k, :], AF.Square))
                sc.inc("scalar", SS)
        else:
            sc.wait("scalar", HW, ld_hw)
            sc.wait("scalar", b.ch(f"ln{block}s"))
            for k in range(DC):
                sc.do("scalar", lambda s, k=k, xsrc=xsrc: s.copy(xbf[:, k, :],
                                                                 xsrc[:, k, :]))
                sc.do("scalar", lambda s, k=k, xsrc=xsrc: s.activation(
                    sqbf[:, k, :], xsrc[:, k, :], AF.Square))
                sc.inc("scalar", SS)
        for k in range(DC):
            sc.wait("tensor", SS, SS.n - (DC - 1 - k))
            sc.do("tensor", lambda t, k=k: t.matmul(
                ps_sum[0:1, :], b.consts["ones_col_bf"].ap(), xbf[:, k, :],
                start=(k == 0), stop=(k == DC - 1)))
            sc.do("tensor", lambda t, k=k: t.matmul(
                ps_sq[0:1, :], b.consts["ones_col_bf"].ap(), sqbf[:, k, :],
                start=(k == 0), stop=(k == DC - 1)))
        sc.inc("tensor", ST)
        sc.wait("scalar", ST)
        sc.do("scalar", lambda s: s.activation(mu, ps_sum[0:1, :], AF.Copy, scale=1.0 / D))
        sc.inc("scalar", SS)
        sc.wait("vector", SS)
        sc.do("vector", lambda v: v.tensor_mul(musq, mu, mu))
        sc.drain("vector")
        sc.do("vector", lambda v: v.tensor_scalar_sub(musq, musq, 1e-5))
        sc.drain("vector")
        sc.do("vector", lambda v: v.scalar_tensor_tensor(
            var, ps_sq[0:1, :], 1.0 / D, musq, ALU.mult, ALU.subtract))
        sc.inc("vector", SV)
        sc.wait("scalar", SV)
        sc.do("scalar", lambda s: s.activation(lnv, var, AF.Ln))
        sc.drain("scalar")
        sc.do("scalar", lambda s: s.activation(rstd, lnv, AF.Exp, scale=-0.5))
        sc.inc("scalar", SS)
        sc.wait("tensor", SS)
        sc.do("tensor", lambda t: t.matmul(ps_mu, b.consts["ones_row_f"].ap(), mu,
                                           start=True, stop=True))
        sc.do("tensor", lambda t: t.matmul(ps_rstd, b.consts["ones_row_f"].ap(), rstd,
                                           start=True, stop=True))
        sc.inc("tensor", ST)
        sc.wait("vector", ST)
        ss_k = [0] * DC
        st_k = [0] * DC
        for k in range(DC):
            if k >= 2:
                sc.wait("vector", SS, ss_k[k - 2])
                if own:
                    sc.wait("vector", ST, st_k[k - 2])
            sc.do("vector", lambda v, k=k, xsrc=xsrc: v.tensor_sub(
                xnA[k % 2], xsrc[:, k, :], ps_mu))
            sc.drain("vector")
            sc.do("vector", lambda v, k=k: v.tensor_mul(xnB[k % 2], xnA[k % 2], ps_rstd))
            svk = sc.inc("vector", SV)
            sc.wait("scalar", SV, svk)
            sc.do("scalar", lambda s, k=k: s.copy(xnbf[:, k, :], xnB[k % 2]))
            ss_k[k] = sc.inc("scalar", SS)
            if own:
                sc.wait("tensor", SV, svk)
                sc.do("tensor", lambda t, k=k: t.matmul(
                    ps_lg[0:nlg, :], lg_til[:, k, 0:nlg], xnB[k % 2],
                    start=(k == 0), stop=(k == DC - 1)))
                st_k[k] = sc.inc("tensor", ST)
        if block == 1 and scn == 0 and "b1dbg" in b.cfg.debug:
            d1 = b.dbg("b1dbg_stats", [1, 4 * SC], F32)
            sc.wait("sync", SS)
            sc.wait("sync", SV)
            sc.dma("sync", d1[:, 0:SC], mu, sem=b.ch("dbg1"))
            sc.dma("sync", d1[:, SC:2 * SC], rstd, sem=b.ch("dbg1"))
            sc.dma("sync", d1[:, 2 * SC:3 * SC], var, sem=b.ch("dbg1"))
            d2 = b.dbg("b1dbg_xin", [128, DC, SC], F32)
            d3 = b.dbg("b1dbg_xnb", [128, SC], F32)
            sc.dma("sync", d2, xin, sem=b.ch("dbg1"))
            sc.dma("sync", d3, xnB[(DC - 1) % 2], sem=b.ch("dbg1"))
        sc.wait("sync", SS, ss_k[DC - 1])
        sc.dma("sync", b.scr_xn[:, scn * SC:(scn + 1) * SC].rearrange(
            "(k p) s -> p k s", p=128), xnbf, sem=b.ch(f"ln{block}s"))
        if own:
            sc.wait("scalar", ST, st_k[DC - 1])
            if block == 0:
                sc.do("scalar", lambda s, scn=scn: s.copy(
                    b.lg_fm.ap()[0:nlg, scn * SC:(scn + 1) * SC], ps_lg[0:nlg, :]))
            else:
                sc.do("scalar", lambda s, scn=scn: s.activation(
                    b.lg_fm.ap()[0:nlg, scn * SC:(scn + 1) * SC], ps_lg[0:nlg, :],
                    AF.Identity, bias=b.consts["gbsum"].ap()))
            lgss = sc.inc("scalar", SS)
            sc.wait("tensor", SS, lgss)
    b.run_phase(name, sc, [("xn", b.scr_xn), ("lg", b.lg_fm.ap()[0:nlg, :])])


def p_fin(b):
    sc = b.sched()
    b.run_phase("fin", sc, final=True)


def per_core_inputs(x, shared, cfg, core):
    bseq, half = core // 2, core % 2
    T, S = cfg.T, cfg.S
    xs = x[bseq].T
    if half == 0:
        xrot = xs
        pos = np.arange(S, dtype=np.float32)
    else:
        xrot = np.concatenate([xs[:, T:], xs[:, :T]], axis=1)
        pos = np.concatenate([np.arange(T, S), np.arange(0, T)]).astype(np.float32)
    d = dict(shared)
    d["xseq"] = np.ascontiguousarray(xrot.astype(np.float32))
    d["pmask"] = np.full((128, 1), NEG if half == 0 else 0.0, dtype=np.float32)
    d["pos_rot"] = pos.reshape(1, S)
    return d


def build_program(cfg, mode="fused", group=None, upto=None):
    b = Builder(cfg, mode, group)
    phases = []
    if mode == "fused" or group == 0:
        phases += [("consts", lambda: p_consts(b)),
                   ("b0_ln", lambda: p_ln_outer(b, 0)),
                   ("b02", lambda: p_b02(b)),
                   ("b03", lambda: p_b03(b)),
                   ("b0_attn", lambda: p_attn(b, 0)),
                   ("b05", lambda: p_b05(b)),
                   ("b067", lambda: p_b067(b)),
                   ("b08", lambda: p_b08(b))]
    if mode == "fused":
        phases += [("cc", lambda: p_cc(b))]
    if mode == "fused" or group == 1:
        phases += [("b1_ln", lambda: p_ln_outer(b, 1)),
                   ("b12", lambda: p_b12(b)),
                   ("b13", lambda: p_b13(b)),
                   ("rope", lambda: p_rope(b)),
                   ("lxn", lambda: p_loadxn(b)),
                   ("b1_attn", lambda: p_attn(b, 1)),
                   ("b16", lambda: p_b16(b))]
    for nm, fn in phases:
        fn()
        if upto == nm:
            break
    p_fin(b)
    return b


def p_b02(b):
    cfg = b.cfg
    T, W = cfg.T, cfg.SC
    NT = T // W
    V = b.view
    o = 0
    lgt = V(o, [128, 3], F32); o += 3 * 4
    m = V(o, [128, 1], F32); o += 4
    ge0 = V(o, [128, 1], F32); o += 4
    ge1 = V(o, [128, 1], F32); o += 4
    nge0 = V(o, [128, 1], F32); o += 4
    nge1 = V(o, [128, 1], F32); o += 4
    sel12 = V(o, [128, 2], F32); o += 2 * 4
    o = 64
    sel1fm = V(o, [1, T], F32); o += T * 4
    sel2fm = V(o, [1, T], F32); o += T * 4
    sel1f = V(o, [128, W], U8); o += W * 2
    t1bf = V(o, [32, W], BF16); o += W * 2
    xnc = V(o, [128, DC, W], BF16); o += DC * W * 2
    o0 = V(o, [128, W], F32); o += W * 4
    o1 = V(o, [128, W], F32); o += W * 4
    g01 = [V(o, [128, W], F32), V(o + W * 4, [128, W], F32)]; o += 2 * W * 4
    ps_t, ps_sel, ps_t1, ps_lr, ps_b = (p.ap() for p in b.ps[:5])
    HW, ST, SV, SS = b.SEM["HW"], b.SEM["T"], b.SEM["V"], b.SEM["S"]
    C = b.consts
    sc = b.sched()
    for tt in range(T // 128):
        cols = slice(tt * 128, (tt + 1) * 128)
        sc.wait("tensor", SV)
        sc.wait("tensor", SS)
        sc.do("tensor", lambda t, cols=cols: t.transpose(
            ps_t[:, 0:3], b.lg_fm.ap()[0:3, cols], C["ident_f32"].ap()[0:3, 0:3]))
        sc.inc("tensor", ST)
        sc.wait("vector", ST)
        sc.do("vector", lambda v: v.tensor_copy(lgt, ps_t[:, 0:3]))
        sc.drain("vector")
        sc.do("vector", lambda v: v.reduce_max(m, lgt, axis=AX))
        sc.drain("vector")
        sc.do("vector", lambda v: v.tensor_scalar(ge0, lgt[:, 0:1], m, None, ALU.is_ge))
        sc.do("vector", lambda v: v.tensor_scalar(ge1, lgt[:, 1:2], m, None, ALU.is_ge))
        sc.drain("vector")
        sc.do("vector", lambda v: v.tensor_scalar(nge0, ge0, -1.0, 1.0, ALU.mult, ALU.add))
        sc.do("vector", lambda v: v.tensor_scalar(nge1, ge1, -1.0, 1.0, ALU.mult, ALU.add))
        sc.drain("vector")
        sc.do("vector", lambda v: v.tensor_mul(sel12[:, 0:1], nge0, ge1))
        sc.do("vector", lambda v: v.tensor_mul(sel12[:, 1:2], nge0, nge1))
        sc.inc("vector", SV)
        sc.wait("tensor", SV)
        sc.do("tensor", lambda t: t.transpose(ps_sel[0:1, 0:128], sel12[:, 0:1],
                                              C["ident_f32"].ap()))
        sc.do("tensor", lambda t: t.transpose(ps_sel[0:1, 128:256], sel12[:, 1:2],
                                              C["ident_f32"].ap()))
        sc.inc("tensor", ST)
        sc.wait("scalar", ST)
        sc.do("scalar", lambda s, cols=cols: s.copy(sel1fm[:, cols], ps_sel[0:1, 0:128]))
        sc.do("scalar", lambda s, cols=cols: s.copy(sel2fm[:, cols], ps_sel[0:1, 128:256]))
        sc.inc("scalar", SS)
    for tc in range(NT):
        cols = slice(tc * W, (tc + 1) * W)
        sc.wait("tensor", SS)
        sc.do("tensor", lambda t, cols=cols: t.matmul(
            ps_b[:, 0:W], C["ones_row_f"].ap(), sel2fm[:, cols], start=True, stop=True))
        sc.inc("tensor", ST)
        sc.wait("scalar", ST)
        sc.do("scalar", lambda s, cols=cols: s.copy(b.sel2f.ap()[:, cols], ps_b[:, 0:W]))
        sc.inc("scalar", SS)
        sc.wait("tensor", SS)
    for tc in range(NT):
        cols = slice(tc * W, (tc + 1) * W)
        sc.do("tensor", lambda t, cols=cols: t.matmul(
            ps_b[:, 0:W], C["ones_row_f"].ap(), sel1fm[:, cols], start=True, stop=True))
        sc.inc("tensor", ST)
        sc.wait("scalar", ST)
        sc.do("scalar", lambda s: s.copy(sel1f, ps_b[:, 0:W]))
        sc.inc("scalar", SS)
        sc.wait("sync", SV)
        sc.wait("sync", SS)
        sc.dma("sync", xnc, b.scr_xn[:, cols].rearrange("(k p) s -> p k s", p=128))
        ld = HW.n
        sc.wait("tensor", HW, ld)
        for k in range(DC):
            sc.do("tensor", lambda t, k=k: t.matmul(
                ps_t1[0:32, 0:W], C["wdown_til"].ap()[:, k, :], xnc[:, k, :],
                start=(k == 0), stop=(k == DC - 1)))
        sc.inc("tensor", ST)
        sc.wait("scalar", ST)
        sc.do("scalar", lambda s: s.copy(t1bf, ps_t1[0:32, 0:W]))
        sc.inc("scalar", SS)
        for k in range(DC):
            sc.wait("tensor", SS)
            sc.wait("tensor", SV)
            sc.do("tensor", lambda t, k=k: t.matmul(
                ps_lr[:, 0:W], C["wup_T"].ap()[0:32, k * 128:(k + 1) * 128], t1bf,
                start=True, stop=True))
            stk = sc.inc("tensor", ST)
            sc.wait("scalar", HW, ld)
            if k >= 1:
                sc.wait("scalar", SV)
            sc.do("scalar", lambda s, k=k: s.activation(
                o0, xnc[:, k, :], AF.Identity, bias=C["sbias_T"].ap()[:, k:k + 1]))
            ssk = sc.inc("scalar", SS)
            sc.wait("vector", ST, stk)
            sc.drain("vector")
            sc.do("vector", lambda v, k=k: v.tensor_add(o1, ps_lr[:, 0:W], xnc[:, k, :]))
            sc.inc("vector", SV)
            sc.wait("vector", SS, ssk)
            if k >= 2:
                sc.wait("vector", b.ch(f"b02g{k % 2}"))
            sc.drain("vector")
            sc.do("vector", lambda v, k=k: v.select(g01[k % 2], sel1f, o1, o0,
                                                    add_drain=True))
            sc.inc("vector", SV)
            sc.wait("sync", SV)
            sc.dma("sync", b.scr_g[k * 128:(k + 1) * 128, cols], g01[k % 2],
                   sem=b.ch(f"b02g{k % 2}"))
    b.run_phase("b02", sc, [("g", b.scr_g), ("sel2f", b.sel2f.ap())])


def xn1_region(b):
    return b.view(0, [128, DC, b.cfg.S], BF16)


def p_b03(b):
    cfg = b.cfg
    S, SC = cfg.S, cfg.SC
    NSC = S // SC
    V = b.view
    XN1 = xn1_region(b)
    o = DC * S * 2
    xnc = V(o, [128, DC, SC], BF16); o += DC * SC * 2
    sqbf = V(o, [128, DC, SC], BF16); o += DC * SC * 2
    xnA = [V(o, [128, SC], F32), V(o + SC * 4, [128, SC], F32)]; o += 2 * SC * 4
    xnB = [V(o, [128, SC], F32), V(o + SC * 4, [128, SC], F32)]; o += 2 * SC * 4
    mu = V(o, [1, SC], F32); o += SC * 4
    musq = V(o, [1, SC], F32); o += SC * 4
    var = V(o, [1, SC], F32); o += SC * 4
    lnv = V(o, [1, SC], F32); o += SC * 4
    rstd = V(o, [1, SC], F32); o += SC * 4
    ps_sum, ps_sq, ps_mu, ps_rstd = (p.ap()[:, 0:SC] for p in b.ps[:4])
    HW, ST, SV, SS = b.SEM["HW"], b.SEM["T"], b.SEM["V"], b.SEM["S"]
    C = b.consts
    sc = b.sched()
    for scn in range(NSC):
        cols = slice(scn * SC, (scn + 1) * SC)
        sc.wait("sync", SV)
        sc.wait("sync", SS)
        sc.dma("sync", xnc, b.scr_xn[:, cols].rearrange("(k p) s -> p k s", p=128))
        ld = HW.n
        sc.wait("scalar", HW, ld)
        for k in range(DC):
            sc.do("scalar", lambda s, k=k: s.activation(sqbf[:, k, :], xnc[:, k, :],
                                                        AF.Square))
            sc.inc("scalar", SS)
        for k in range(DC):
            sc.wait("tensor", SS, SS.n - (DC - 1 - k))
            if k == 0:
                sc.wait("tensor", HW, ld)
            sc.do("tensor", lambda t, k=k: t.matmul(
                ps_sum[0:1, :], C["ones_col_bf"].ap(), xnc[:, k, :],
                start=(k == 0), stop=(k == DC - 1)))
            sc.do("tensor", lambda t, k=k: t.matmul(
                ps_sq[0:1, :], C["ones_col_bf"].ap(), sqbf[:, k, :],
                start=(k == 0), stop=(k == DC - 1)))
        sc.inc("tensor", ST)
        sc.wait("scalar", ST)
        sc.do("scalar", lambda s: s.activation(mu, ps_sum[0:1, :], AF.Copy, scale=1.0 / D))
        sc.inc("scalar", SS)
        sc.wait("vector", SS)
        sc.do("vector", lambda v: v.tensor_mul(musq, mu, mu))
        sc.drain("vector")
        sc.do("vector", lambda v: v.tensor_scalar_sub(musq, musq, 1e-5))
        sc.drain("vector")
        sc.do("vector", lambda v: v.scalar_tensor_tensor(
            var, ps_sq[0:1, :], 1.0 / D, musq, ALU.mult, ALU.subtract))
        sc.inc("vector", SV)
        sc.wait("scalar", SV)
        sc.do("scalar", lambda s: s.activation(lnv, var, AF.Ln))
        sc.drain("scalar")
        sc.do("scalar", lambda s: s.activation(rstd, lnv, AF.Exp, scale=-0.5))
        sc.inc("scalar", SS)
        sc.wait("tensor", SS)
        sc.do("tensor", lambda t: t.matmul(ps_mu, C["ones_row_f"].ap(), mu,
                                           start=True, stop=True))
        sc.do("tensor", lambda t: t.matmul(ps_rstd, C["ones_row_f"].ap(), rstd,
                                           start=True, stop=True))
        sc.inc("tensor", ST)
        sc.wait("vector", ST)
        ss_k = [0] * DC
        for k in range(DC):
            if k >= 2:
                sc.wait("vector", SS, ss_k[k - 2])
            sc.do("vector", lambda v, k=k: v.tensor_sub(xnA[k % 2], xnc[:, k, :], ps_mu))
            sc.drain("vector")
            sc.do("vector", lambda v, k=k: v.tensor_mul(xnB[k % 2], xnA[k % 2], ps_rstd))
            svk = sc.inc("vector", SV)
            sc.wait("scalar", SV, svk)
            sc.do("scalar", lambda s, k=k, cols=cols: s.activation(
                XN1[:, k, cols], xnB[k % 2], AF.Identity,
                scale=C["ln1_scale"].ap()[:, k:k + 1], bias=C["ln1_bias"].ap()[:, k:k + 1]))
            ss_k[k] = sc.inc("scalar", SS)
    b.run_phase("b03", sc, [("xn1", XN1)])


def rope_region(b):
    off = b.ARN - 2 * b.cfg.S * 4
    return (b.view(off, [64, b.cfg.S], F32),
            b.view(off + b.cfg.S * 4, [64, b.cfg.S], F32))


def p_attn(b, block):
    cfg = b.cfg
    S, T, W = cfg.S, cfg.T, cfg.SC
    KT = S // 128
    NQC = T // W
    NKC = S // W
    POWN = T // 128
    BAND = W // 128
    V = b.view
    XN1 = xn1_region(b)
    o = DC * S * 2
    vgrp = V(o, [128, KT, 512], BF16); o += KT * 512 * 2
    kh = V(o, [128, S], BF16); o += S * 2
    qh = V(o, [128, T], BF16); o += T * 2
    ee = [V(o, [128, W], BF16), V(o + W * 2, [128, W], BF16)]; o += 2 * W * 2
    ssc = V(o, [128, W], F32); o += W * 4
    rr = V(o, [1, W], F32); o += W * 4
    ctxs = V(o, [128, W], F32); o += W * 4
    qs = [V(o, [128, W], F32), V(o + W * 4, [128, W], F32)]; o += 2 * W * 4
    cosv, sinv = rope_region(b)
    assert o <= b.ARN - (2 * S * 4 if block == 1 else 0)
    ps_v, ps_k, ps_q, ps_s0, ps_s1, ps_ctx, ps_r, ps_b = (p.ap() for p in b.ps)
    HW, ST, SV, SS = b.SEM["HW"], b.SEM["T"], b.SEM["V"], b.SEM["S"]
    C = b.consts
    wv_h = b.hin["l0wv_til" if block == 0 else "l1wv_til"]
    wk_h = b.hin["l0wk_til" if block == 0 else "l1wk_til"]
    wq_h = b.hin["l0wq_til" if block == 0 else "l1wq_til"]
    invsq = 1.0 / math.sqrt(DH)
    sc = b.sched()

    def rope_evac(ps_in, out_bf, cols_pos):
        cosc = cosv[:, cols_pos]
        sinc = sinv[:, cols_pos]
        a, bb = qs
        sc.drain("vector")
        sc.do("vector", lambda v: v.tensor_mul(a[0:64, :], ps_in[0:64, :], cosc))
        sc.do("vector", lambda v: v.tensor_mul(bb[0:64, :], ps_in[64:128, :], sinc))
        sc.drain("vector")
        sc.do("vector", lambda v: v.tensor_sub(out_bf[0:64, :], a[0:64, :], bb[0:64, :]))
        sc.do("vector", lambda v: v.tensor_mul(a[64:128, :], ps_in[0:64, :], sinc))
        sc.do("vector", lambda v: v.tensor_mul(bb[64:128, :], ps_in[64:128, :], cosc))
        sc.drain("vector")
        sc.do("vector", lambda v: v.tensor_add(out_bf[64:128, :], a[64:128, :],
                                               bb[64:128, :]))
        sc.inc("vector", SV)

    vproj_free = [0, 0]
    kq_free = [None, None]
    for g in range(NH // 4):
        heads = list(range(4 * g, 4 * g + 4))
        hv = b.wplan([wv_h[h] for h in heads], quad=True)
        hks, hqs = {}, {}
        for h in heads:
            hks[h] = b.wplan([wk_h[h]])[0]
            hqs[h] = b.wplan([wq_h[h]])[0]
        sc.wloads(hv + [x for h in heads for x in (hks[h], hqs[h])])
        quad = [h for h in hv if not h.null][0]
        for h_ in hv:
            sc.wwait(h_)
        for tt in range(KT):
            bk = [ps_v, ps_k][tt % 2]
            if vproj_free[tt % 2]:
                sc.wait("tensor", SS, vproj_free[tt % 2])
            for k in range(DC):
                sc.do("tensor", lambda t, k=k, tt=tt, quad=quad, bk=bk: t.matmul(
                    bk[:, :], XN1[:, k, tt * 128:(tt + 1) * 128],
                    b.wring[:, quad.slot:quad.slot + 4, k, :],
                    start=(k == 0), stop=(k == DC - 1)))
            ins_i = sc.inc("tensor", ST)
            sc.wait("scalar", ST, ins_i)
            sc.do("scalar", lambda s, tt=tt, bk=bk: s.copy(vgrp[:, tt, :], bk[:, :]))
            if tt == KT - 1:
                for _h in hv:
                    sc.inc("scalar", b.SEM["WF"], 1)
            vproj_free[tt % 2] = sc.inc("scalar", SS)
        kq_free[0] = (SS.h, vproj_free[1])
        for hh, h in enumerate(heads):
            sc.wwait(hks[h])
            for sct in range(NKC):
                cols = slice(sct * W, (sct + 1) * W)
                bk = [ps_k, ps_q][sct % 2]
                if kq_free[sct % 2]:
                    sc.q["tensor"].append(("wait",) + kq_free[sct % 2])
                for k in range(DC):
                    sc.do("tensor", lambda t, k=k, hk=hks[h], cols=cols, bk=bk: t.matmul(
                        bk[:, 0:W], b.wview(hk, k), XN1[:, k, cols],
                        start=(k == 0), stop=(k == DC - 1)))
                kst = sc.inc("tensor", ST)
                if block == 0:
                    sc.wait("scalar", ST, kst)
                    sc.do("scalar", lambda s, cols=cols, bk=bk: s.copy(kh[:, cols],
                                                                       bk[:, 0:W]))
                    if sct == NKC - 1:
                        sc.inc("scalar", b.SEM["WF"], 1)
                    kq_free[sct % 2] = (SS.h, sc.inc("scalar", SS))
                else:
                    sc.wait("vector", ST, kst)
                    rope_evac(bk[:, 0:W], kh[:, cols], cols)
                    if sct == NKC - 1:
                        sc.inc("vector", b.SEM["WF"], 1)
                    kq_free[sct % 2] = (SV.h, SV.n)
            sc.wwait(hqs[h])
            for qct in range(NQC):
                cols = slice(qct * W, (qct + 1) * W)
                bk = [ps_k, ps_q][qct % 2]
                if kq_free[qct % 2]:
                    sc.q["tensor"].append(("wait",) + kq_free[qct % 2])
                for k in range(DC):
                    sc.do("tensor", lambda t, k=k, hq=hqs[h], cols=cols, bk=bk: t.matmul(
                        bk[:, 0:W], b.wview(hq, k), XN1[:, k, cols],
                        start=(k == 0), stop=(k == DC - 1)))
                qst = sc.inc("tensor", ST)
                if block == 0:
                    sc.wait("scalar", ST, qst)
                    sc.do("scalar", lambda s, cols=cols, bk=bk: s.copy(qh[:, cols],
                                                                       bk[:, 0:W]))
                    if qct == NQC - 1:
                        sc.inc("scalar", b.SEM["WF"], 1)
                    kq_free[qct % 2] = (SS.h, sc.inc("scalar", SS))
                else:
                    sc.wait("vector", ST, qst)
                    rope_evac(bk[:, 0:W], qh[:, cols], cols)
                    if qct == NQC - 1:
                        sc.inc("vector", b.SEM["WF"], 1)
                    kq_free[qct % 2] = (SV.h, SV.n)
            sc.wait("tensor", SS)
            sc.wait("tensor", SV)
            for qc in range(NQC):
                qcols = slice(qc * W, (qc + 1) * W)
                own_tiles = list(range(0, (qc + 1) * BAND))
                tiles = own_tiles + list(range(POWN, KT))
                n = len(tiles)
                psfree = [None] * n
                expdone = [0] * n
                avdone = [0] * n
                prev_band_exp = [0]

                def emit_av(i):
                    first = (i == 0)
                    last = (i == n - 1)
                    sc.wait("tensor", SS, expdone[i])
                    kt = tiles[i]
                    sc.do("tensor", lambda t, kt=kt, hh=hh, i=i, first=first, last=last:
                          t.matmul(ps_ctx[:, 0:W],
                                   vgrp[:, kt, hh * 128:(hh + 1) * 128], ee[i % 2],
                                   start=first, stop=last))
                    sc.do("tensor", lambda t, i=i, first=first, last=last: t.matmul(
                        ps_r[0:1, 0:W], C["ones_col_bf"].ap(), ee[i % 2],
                        start=first, stop=last))
                    avdone[i] = sc.inc("tensor", ST)

                for i, kt in enumerate(tiles):
                    ps_s = ps_s0 if i % 2 == 0 else ps_s1
                    if i >= 2:
                        sc.q["tensor"].append(("wait",) + psfree[i - 2])
                    sc.do("tensor", lambda t, kt=kt, ps_s=ps_s, qcols=qcols: t.matmul(
                        ps_s[:, 0:W], kh[:, kt * 128:(kt + 1) * 128], qh[:, qcols],
                        start=True, stop=True))
                    smst = sc.inc("tensor", ST)
                    bi = kt - qc * BAND
                    if i >= 2:
                        e_guard = [("tensor", ST, avdone[i - 2])]
                    else:
                        e_guard = []
                    if kt in own_tiles and 0 <= bi < BAND:
                        sc.wait("vector", ST, smst)
                        if prev_band_exp[0]:
                            sc.wait("vector", SS, prev_band_exp[0])
                        sc.drain("vector")
                        sc.do("vector", lambda v, ps_s=ps_s, bi=bi: v.tensor_add(
                            ssc, ps_s[:, 0:W], C["maskband"].ap()[:, bi, :]))
                        bsv = sc.inc("vector", SV)
                        psfree[i] = (SS.h, 0)
                        psfree[i] = (SV.h, bsv)
                        sc.wait("scalar", SV, bsv)
                        for gname, gh, gv in e_guard:
                            sc.wait("scalar", b.SEM["T"], gv)
                        sc.do("scalar", lambda s, eidx=i % 2: s.activation(
                            ee[eidx], ssc, AF.Exp, scale=invsq, bias=C["cshift"].ap()))
                        expdone[i] = sc.inc("scalar", SS)
                        prev_band_exp[0] = expdone[i]
                    else:
                        bias_ap = C["cshift"] if kt in own_tiles else C["pm_c"]
                        sc.wait("scalar", ST, smst)
                        for gname, gh, gv in e_guard:
                            sc.wait("scalar", b.SEM["T"], gv)
                        sc.do("scalar", lambda s, eidx=i % 2, ps_s=ps_s, bias_ap=bias_ap:
                              s.activation(ee[eidx], ps_s[:, 0:W], AF.Exp,
                                           scale=invsq, bias=bias_ap.ap()))
                        expdone[i] = sc.inc("scalar", SS)
                        psfree[i] = (SS.h, expdone[i])
                    if i >= 1:
                        emit_av(i - 1)
                emit_av(n - 1)
                sc.wait("vector", ST)
                sc.do("vector", lambda v: v.reciprocal(rr, ps_r[0:1, 0:W]))
                sc.inc("vector", SV)
                sc.wait("tensor", SV)
                sc.do("tensor", lambda t: t.matmul(ps_b[:, 0:W], C["ones_row_f"].ap(),
                                                   rr, start=True, stop=True))
                bst = sc.inc("tensor", ST)
                sc.wait("scalar", ST, bst)
                sc.do("scalar", lambda s: s.copy(ctxs, ps_ctx[:, 0:W]))
                csk = sc.inc("scalar", SS)
                sc.wait("vector", SS, csk)
                sc.do("vector", lambda v, h=h, qcols=qcols: v.tensor_mul(
                    b.ctx_all[:, h, qcols], ctxs, ps_b[:, 0:W]))
                sc.inc("vector", SV)
                sc.wait("tensor", SV)
    b.run_phase(f"b{block}_attn", sc, [("ctx", b.ctx_all.ap())])


def p_b05(b):
    cfg = b.cfg
    T, W = cfg.T, cfg.SC
    NT = T // W
    V = b.view
    o = 0
    xnc = V(o, [128, DC, T], BF16); o += DC * T * 2
    ybuf = [V(o, [128, W], F32), V(o + W * 4, [128, W], F32)]; o += 2 * W * 4
    ps_y = [b.ps[0].ap(), b.ps[1].ap()]
    HW, ST, SV, SS = (b.SEM[x] for x in ("HW", "T", "V", "S"))
    sc = b.sched()
    hw_chunks = b.wplan([b.hin["l0wo_til"][k] for k in range(DC)])
    sc.wloads(hw_chunks)
    chx = b.ch("b05x")
    for tc in range(NT):
        cols = slice(tc * W, (tc + 1) * W)
        sc.dma("sync", xnc[:, :, cols],
               b.scr_xn[:, cols].rearrange("(k p) s -> p k s", p=128), sem=chx)
    sc.wait("tensor", chx)
    sc.wait("vector", chx)
    svslot = [0, 0]
    for k in range(DC):
        sc.wwait(hw_chunks[k])
        for tc in range(NT):
            cols = slice(tc * W, (tc + 1) * W)
            i2 = (k * NT + tc) % 2
            if svslot[i2]:
                sc.wait("tensor", SV, svslot[i2])
            for hc in range(DC):
                sc.do("tensor", lambda t, k=k, hc=hc, cols=cols, i2=i2: t.matmul(
                    ps_y[i2][:, 0:W], b.wview(hw_chunks[k], hc),
                    b.ctx_all[:, hc, cols], start=(hc == 0), stop=(hc == DC - 1)))
            mmk = sc.inc("tensor", ST)
            sc.wait("vector", ST, mmk)
            sc.wait("vector", b.ch(f"b05y{i2}"))
            sc.drain("vector")
            sc.do("vector", lambda v, k=k, i2=i2, cols=cols: v.tensor_add(
                ybuf[i2], ps_y[i2][:, 0:W], xnc[:, k, cols]))
            if tc == NT - 1:
                sc.inc("vector", b.SEM["WF"], 1)
            svslot[i2] = sc.inc("vector", SV)
            sc.wait("sync", SV, svslot[i2])
            sc.dma("sync", b.scr_y[k * 128:(k + 1) * 128, cols], ybuf[i2],
                   sem=b.ch(f"b05y{i2}"))
    b.run_phase("b05", sc, [("y", b.scr_y)])


def p_b067(b, stages=3):
    cfg = b.cfg
    T, W = cfg.T, cfg.SC
    NT = T // W
    V = b.view
    o = 0
    yn2 = V(o, [128, DC, T], BF16); o += DC * T * 2
    h1h = V(o, [128, 32, T], BF16); o += 32 * T * 2
    yk = [V(o, [128, W], F32), V(o + W * 4, [128, W], F32)]; o += 2 * W * 4
    oB = o
    ybf = [V(oB, [128, W], BF16), V(oB + W * 2, [128, W], BF16)]
    sqbf = [V(oB + 2 * W * 2, [128, W], BF16), V(oB + 3 * W * 2, [128, W], BF16)]
    xnA = V(oB + 4 * W * 2, [128, W], F32)
    xnB = V(oB + 4 * W * 2 + W * 4, [128, W], F32)
    o2 = oB + 4 * W * 2 + 2 * W * 4
    mu = V(o2, [1, W], F32)
    musq = V(o2 + W * 4, [1, W], F32)
    var = V(o2 + 2 * W * 4, [1, W], F32)
    lnv = V(o2 + W * 4, [1, W], F32)
    rstd = V(o2 + 2 * W * 4, [1, W], F32)
    gold = [V(oB, [128, W], F32), V(oB + W * 4, [128, W], F32)]
    gsel = [V(oB + 2 * W * 4, [128, W], F32), V(oB + 3 * W * 4, [128, W], F32)]
    fpart = [V(oB + 4 * W * 4, [128, W], F32), V(oB + 5 * W * 4, [128, W], F32)]
    o2b = V(oB + 6 * W * 4, [128, W], F32)
    endB = oB + max(4 * W * 2 + 2 * W * 4 + 3 * W * 4, 7 * W * 4)
    assert endB <= b.ARN, endB
    ps_sum, ps_sq, ps_mu, ps_rstd = (p.ap()[:, 0:W] for p in b.ps[:4])
    ps_f = [b.ps[4].ap(), b.ps[5].ap()]
    ps_o = [b.ps[6].ap(), b.ps[7].ap()]
    HW, HWS, ST, SV, SS = (b.SEM[x] for x in ("HW", "HWS", "T", "V", "S"))
    C = b.consts
    sc = b.sched()
    ff1_h = b.hin["ff1_til"]
    ff2_h = b.hin["ff2_til"]
    for tc in range(NT):
        cols = slice(tc * W, (tc + 1) * W)
        ssl = [0, 0]
        stk = [0, 0]
        for k in range(DC):
            sem_k = b.SEM["HA"] if k % 2 == 0 else b.SEM["HB"]
            if k >= 2:
                sc.wait("sync", SS, ssl[k % 2])
            sc.dma("sync", yk[k % 2], b.scr_y[k * 128:(k + 1) * 128, cols], sem=sem_k)
            ldk = sem_k.n
            sc.wait("scalar", sem_k, ldk)
            if k >= 2:
                sc.wait("scalar", ST, stk[k % 2])
            sc.do("scalar", lambda s, k=k: s.copy(ybf[k % 2], yk[k % 2]))
            sc.drain("scalar")
            sc.do("scalar", lambda s, k=k: s.activation(sqbf[k % 2], yk[k % 2], AF.Square))
            ssl[k % 2] = sc.inc("scalar", SS)
            sc.wait("tensor", SS, ssl[k % 2])
            sc.do("tensor", lambda t, k=k: t.matmul(
                ps_sum[0:1, :], C["ones_col_bf"].ap(), ybf[k % 2],
                start=(k == 0), stop=(k == DC - 1)))
            sc.do("tensor", lambda t, k=k: t.matmul(
                ps_sq[0:1, :], C["ones_col_bf"].ap(), sqbf[k % 2],
                start=(k == 0), stop=(k == DC - 1)))
            stk[k % 2] = sc.inc("tensor", ST)
        sc.wait("scalar", ST)
        sc.do("scalar", lambda s: s.activation(mu, ps_sum[0:1, :], AF.Copy, scale=1.0 / D))
        sc.inc("scalar", SS)
        sc.wait("vector", SS)
        sc.do("vector", lambda v: v.tensor_mul(musq, mu, mu))
        sc.drain("vector")
        sc.do("vector", lambda v: v.tensor_scalar_sub(musq, musq, 1e-5))
        sc.drain("vector")
        sc.do("vector", lambda v: v.scalar_tensor_tensor(
            var, ps_sq[0:1, :], 1.0 / D, musq, ALU.mult, ALU.subtract))
        sc.inc("vector", SV)
        sc.wait("scalar", SV)
        sc.do("scalar", lambda s: s.activation(lnv, var, AF.Ln))
        sc.drain("scalar")
        sc.do("scalar", lambda s: s.activation(rstd, lnv, AF.Exp, scale=-0.5))
        sc.inc("scalar", SS)
        sc.wait("tensor", SS)
        sc.do("tensor", lambda t: t.matmul(ps_mu, C["ones_row_f"].ap(), mu,
                                           start=True, stop=True))
        sc.do("tensor", lambda t: t.matmul(ps_rstd, C["ones_row_f"].ap(), rstd,
                                           start=True, stop=True))
        sc.inc("tensor", ST)
        sc.wait("vector", ST)
        for k in range(DC):
            sc.wait("sync", SV)
            cha = b.ch(f"b067a{k % 2}")
            sc.dma("sync", yk[k % 2], b.scr_y[k * 128:(k + 1) * 128, cols], sem=cha)
            sc.wait("vector", cha)
            sc.wait("vector", SS)
            sc.drain("vector")
            sc.do("vector", lambda v, k=k: v.tensor_sub(xnA, yk[k % 2], ps_mu))
            sc.drain("vector")
            sc.do("vector", lambda v: v.tensor_mul(xnB, xnA, ps_rstd))
            sc.inc("vector", SV)
            sc.wait("scalar", SV)
            sc.do("scalar", lambda s, k=k, cols=cols: s.activation(
                yn2[:, k, cols], xnB, AF.Identity,
                scale=C["ln2_scale"].ap()[:, k:k + 1], bias=C["ln2_bias"].ap()[:, k:k + 1]))
            sc.inc("scalar", SS)
    for half in range(2):
        h_ff1 = b.wplan([ff1_h[32 * half + j] for j in range(32)])
        h_ff2 = {}
        for k in range(DC):
            h_ff2[k] = b.wplan([ff2_h[k][:, 32 * half + 16 * j:32 * half + 16 * (j + 1), :]
                                for j in range(2)])
        sc.wloads(h_ff1 + [x for k in range(DC) for x in h_ff2[k]])
        sc.wait("tensor", SS)
        evff = [0, 0]
        for j in range(32):
            sc.wwait(h_ff1[j])
            for tc in range(NT):
                i2 = (j * NT + tc) % 2
                cols = slice(tc * W, (tc + 1) * W)
                if evff[i2]:
                    sc.wait("tensor", SS, evff[i2])
                for k in range(DC):
                    sc.do("tensor", lambda t, hw=h_ff1[j], k=k, i2=i2, cols=cols: t.matmul(
                        ps_f[i2][:, 0:W], b.wview(hw, k), yn2[:, k, cols],
                        start=(k == 0), stop=(k == DC - 1)))
                fst = sc.inc("tensor", ST)
                sc.wait("scalar", ST, fst)
                sc.do("scalar", lambda s, j=j, i2=i2, cols=cols: s.activation(
                    h1h[:, j, cols], ps_f[i2][:, 0:W], AF.Gelu_apprx_tanh))
                if tc == NT - 1:
                    sc.inc("scalar", b.SEM["WF"], 1)
                evff[i2] = sc.inc("scalar", SS)
        sc.wait("tensor", SS)
        ofree = [None, None]
        for k in range(DC):
            for j in range(2):
                sc.wwait(h_ff2[k][j])
            for tc in range(NT):
                cols = slice(tc * W, (tc + 1) * W)
                i2 = (k * NT + tc) % 2
                if ofree[i2]:
                    sc.q["tensor"].append(("wait",) + ofree[i2])
                for hh in range(32):
                    sc.do("tensor", lambda t, hw=h_ff2[k][hh // 16], hh=hh, i2=i2,
                          cols=cols: t.matmul(
                        ps_o[i2][:, 0:W], b.wview(hw, hh % 16), h1h[:, hh, cols],
                        start=(hh == 0), stop=(hh == 31)))
                okk = sc.inc("tensor", ST)
                chf = b.ch(f"f2f{i2}")
                if half == 0:
                    sc.wait("scalar", ST, okk)
                    sc.wait("scalar", chf)
                    sc.do("scalar", lambda s, i2=i2: s.copy(fpart[i2], ps_o[i2][:, 0:W]))
                    if tc == NT - 1:
                        sc.inc("scalar", b.SEM["WF"], 1)
                        sc.inc("scalar", b.SEM["WF"], 1)
                    ofree[i2] = (SS.h, sc.inc("scalar", SS))
                    sc.wait("sync", SS)
                    sc.dma("sync", b.scr_f[k * 128:(k + 1) * 128, cols], fpart[i2],
                           sem=chf)
                else:
                    chy = b.ch(f"f2y{i2}")
                    chg = b.ch(f"f2g{i2}")
                    sc.wait("sync", SV)
                    sc.wait("sync", chf)
                    sc.dma("sync", yk[i2], b.scr_y[k * 128:(k + 1) * 128, cols], sem=chy)
                    sc.dma("sync", gold[i2], b.scr_g[k * 128:(k + 1) * 128, cols], sem=chg)
                    sc.dma("sync", fpart[i2], b.scr_f[k * 128:(k + 1) * 128, cols],
                           sem=chf)
                    sc.wait("vector", ST, okk)
                    sc.wait("vector", chy)
                    sc.wait("vector", chg)
                    sc.wait("vector", chf)
                    sc.wait("vector", b.ch(f"f2s{i2}"))
                    sc.drain("vector")
                    sc.do("vector", lambda v, i2=i2: v.tensor_add(o2b, ps_o[i2][:, 0:W],
                                                                  fpart[i2]))
                    sc.drain("vector")
                    sc.do("vector", lambda v, i2=i2: v.tensor_add(gsel[i2], o2b, yk[i2]))
                    sc.drain("vector")
                    sc.do("vector", lambda v, i2=i2, cols=cols: v.select(
                        o2b, b.sel2f.ap()[:, cols], gsel[i2], gold[i2], add_drain=True))
                    sc.drain("vector")
                    sc.do("vector", lambda v, i2=i2: v.tensor_copy(gsel[i2], o2b))
                    if tc == NT - 1:
                        sc.inc("vector", b.SEM["WF"], 1)
                        sc.inc("vector", b.SEM["WF"], 1)
                    ofree[i2] = (SV.h, sc.inc("vector", SV))
                    sc.wait("sync", SV)
                    sc.dma("sync", b.scr_g[k * 128:(k + 1) * 128, cols], gsel[i2],
                           sem=b.ch(f"f2s{i2}"))
    b.run_phase("b067", sc, [("g", b.scr_g)])


def p_b08(b):
    cfg = b.cfg
    T, W = cfg.T, cfg.SC
    NT = T // W
    V = b.view
    o = 0
    xnc = V(o, [128, DC, T], BF16); o += DC * T * 2
    vsum_bf = V(o, [128, T], BF16); o += T * 2
    sbuf_s = V(o, [128, W], F32); o += W * 4
    gbuf = [V(o, [128, W], F32), V(o + W * 4, [128, W], F32)]; o += 2 * W * 4
    gs = V(o, [128, W], F32); o += W * 4
    xo = [V(o, [128, W], F32), V(o + W * 4, [128, W], F32)]; o += 2 * W * 4
    prod = V(o, [128, W], F32); o += W * 4
    xnew = [V(o, [128, W], F32), V(o + W * 4, [128, W], F32)]; o += 2 * W * 4
    assert o <= b.ARN
    ps_v = [b.ps[0].ap(), b.ps[1].ap()]
    ps_s = [b.ps[2].ap(), b.ps[3].ap()]
    HW, HWS, ST, SV, SS = (b.SEM[x] for x in ("HW", "HWS", "T", "V", "S"))
    C = b.consts
    sc = b.sched()
    hv0 = b.wplan([b.hin["wv0_til"][c] for c in range(DC)])
    hwo = b.wplan([b.hin["wout_til"][k] for k in range(DC)])
    sc.wloads(hv0 + hwo)
    chx = b.ch("b08x")
    for tc in range(NT):
        cols = slice(tc * W, (tc + 1) * W)
        sc.dma("sync", xnc[:, :, cols],
               b.scr_xn[:, cols].rearrange("(k p) s -> p k s", p=128), sem=chx)
    sc.wait("tensor", chx)
    for c in range(DC):
        sc.wwait(hv0[c])
        for tc in range(NT):
            for k in range(DC):
                sc.do("tensor", lambda t, c=c, tc=tc, k=k: t.matmul(
                    ps_v[tc % 2][:, 0:W], b.wview(hv0[c], k),
                    xnc[:, k, tc * W:(tc + 1) * W],
                    start=(c == 0 and k == 0), stop=(c == DC - 1 and k == DC - 1)))
        sc.inc("tensor", b.SEM["WF"], 1)
    sc.inc("tensor", ST)
    sc.wait("scalar", ST)
    for tc in range(NT):
        sc.do("scalar", lambda s, tc=tc: s.activation(
            vsum_bf[:, tc * W:(tc + 1) * W], ps_v[tc % 2][:, 0:W], AF.Identity,
            bias=C["bvsum"].ap()))
        sc.inc("scalar", SS)
    sc.wait("tensor", SS)
    for k in range(DC):
        sc.wwait(hwo[k])
        for tc in range(NT):
            cols = slice(tc * W, (tc + 1) * W)
            i2 = (k * NT + tc) % 2
            sc.wait("tensor", SS)
            for hc in range(DC):
                sc.do("tensor", lambda t, k=k, hc=hc, tc=tc, i2=i2: t.matmul(
                    ps_s[i2][:, 0:W], b.wview(hwo[k], hc),
                    vsum_bf[:, tc * W:(tc + 1) * W],
                    start=(hc == 0), stop=(hc == DC - 1)))
            if tc == NT - 1:
                sc.inc("tensor", b.SEM["WF"], 1)
            mmk = sc.inc("tensor", ST)
            sc.wait("sync", SV)
            sc.wait("sync", b.ch(f"b08s{i2}"))
            chg8 = b.ch(f"b08g{i2}")
            cho8 = b.ch(f"b08o{i2}")
            sc.dma("sync", gbuf[i2], b.scr_g[k * 128:(k + 1) * 128, cols], sem=chg8)
            sc.dma("sync", xo[i2], b.hin["xseq"][k * 128:(k + 1) * 128, cols], sem=cho8)
            sc.wait("scalar", ST, mmk)
            sc.wait("scalar", chg8)
            sc.wait("vector", cho8)
            sc.drain("scalar")
            sc.do("scalar", lambda s, i2=i2: s.copy(sbuf_s, ps_s[i2][:, 0:W]))
            sc.drain("scalar")
            sc.do("scalar", lambda s, i2=i2: s.activation(gs, gbuf[i2], AF.Silu))
            sc.inc("scalar", SS)
            sc.wait("vector", SS)
            sc.wait("vector", b.ch(f"b08s{i2}"))
            sc.drain("vector")
            sc.do("vector", lambda v, k=k: v.tensor_scalar_add(
                prod, sbuf_s, C["bout_T"].ap()[:, k:k + 1]))
            sc.drain("vector")
            sc.do("vector", lambda v, i2=i2: v.tensor_mul(xnew[i2], prod, gs))
            sc.drain("vector")
            sc.do("vector", lambda v, i2=i2: v.tensor_add(gbuf[i2], xnew[i2], xo[i2]))
            sc.inc("vector", SV)
            sc.wait("sync", SV)
            sc.dma("sync", b.cc_in.ap()[k * 128:(k + 1) * 128, cols], gbuf[i2],
                   sem=b.ch(f"b08s{i2}"))
    b.run_phase("b08", sc, [("x1", b.cc_in.ap())])


def p_cc(b):
    sc = b.sched()
    groups = [[2 * i, 2 * i + 1] for i in range(b.cfg.ncores // 2)]
    CC = b.SEM["CC"]
    for i2 in range(2):
        sc.wait("gpsimd", b.ch(f"b08s{i2}"))
    sc.do("gpsimd", lambda g: g.collective_compute(
        "AllReduce", ALU.add, replica_groups=groups,
        ins=[b.cc_in.ap()], outs=[b.cc_out.ap()]))
    sc.inc("gpsimd", CC)
    b.run_phase("cc", sc)


def p_b12(b):
    cfg = b.cfg
    T = cfg.T
    BIG = 1.0e30
    V = b.view
    o = 0
    lgt = V(o, [128, 8], F32); o += 8 * 4
    m1 = V(o, [128, 1], F32); o += 4
    m2 = V(o, [128, 1], F32); o += 4
    eq1 = V(o, [128, 8], F32); o += 8 * 4
    eq2 = V(o, [128, 8], F32); o += 8 * 4
    msk = V(o, [128, 8], F32); o += 8 * 4
    dlt = V(o, [128, 1], F32); o += 4
    g2 = V(o, [128, 1], F32); o += 4
    g1 = V(o, [128, 1], F32); o += 4
    t1 = V(o, [128, 8], F32); o += 8 * 4
    comb = V(o, [128, 8], F32); o += 8 * 4
    ps_t, ps_c, ps_c2 = b.ps[0].ap(), b.ps[1].ap(), b.ps[2].ap()
    ST, SV, SS = b.SEM["T"], b.SEM["V"], b.SEM["S"]
    C = b.consts
    sc = b.sched()
    for tt in range(T // 128):
        cols = slice(tt * 128, (tt + 1) * 128)
        sc.wait("tensor", SV)
        sc.wait("tensor", SS)
        sc.do("tensor", lambda t, cols=cols: t.transpose(
            ps_t[:, 0:8], b.lg_fm.ap()[0:8, cols], C["ident_f32"].ap()[0:8, 0:8]))
        sc.inc("tensor", ST)
        sc.wait("vector", ST)
        sc.do("vector", lambda v: v.tensor_copy(lgt, ps_t[:, 0:8]))
        sc.drain("vector")
        sc.do("vector", lambda v: v.reduce_max(m1, lgt, axis=AX))
        sc.drain("vector")
        sc.do("vector", lambda v: v.tensor_scalar(eq1, lgt, m1, None, ALU.is_ge))
        sc.drain("vector")
        sc.do("vector", lambda v: v.scalar_tensor_tensor(msk, eq1, -BIG, lgt,
                                                         ALU.mult, ALU.add))
        sc.drain("vector")
        sc.do("vector", lambda v: v.reduce_max(m2, msk, axis=AX))
        sc.drain("vector")
        sc.do("vector", lambda v: v.tensor_sub(dlt, m2, m1))
        sc.inc("vector", SV)
        sc.wait("scalar", SV)
        sc.do("scalar", lambda s: s.activation(g2, dlt, AF.Sigmoid))
        sc.inc("scalar", SS)
        sc.wait("vector", SS)
        sc.do("vector", lambda v: v.tensor_scalar(g1, g2, -1.0, 1.0, ALU.mult, ALU.add))
        sc.do("vector", lambda v: v.tensor_scalar(eq2, msk, m2, None, ALU.is_ge))
        sc.drain("vector")
        sc.do("vector", lambda v: v.tensor_scalar(t1, eq1, g1, None, ALU.mult))
        sc.drain("vector")
        sc.do("vector", lambda v: v.scalar_tensor_tensor(comb, eq2, g2, t1,
                                                         ALU.mult, ALU.add))
        sc.inc("vector", SV)
        sc.wait("tensor", SV)
        sc.do("tensor", lambda t: t.transpose(ps_c[0:8, 0:128], comb,
                                              C["ident_f32"].ap()))
        sc.inc("tensor", ST)
        sc.wait("scalar", ST)
        sc.do("scalar", lambda s, cols=cols: s.copy(b.comb_fm.ap()[:, cols],
                                                    ps_c[0:8, 0:128]))
        sc.inc("scalar", SS)
    b.run_phase("b12", sc, [("comb", b.comb_fm.ap())])



def p_b13(b):
    cfg = b.cfg
    T, W = cfg.T, cfg.SC
    NT = T // W
    V = b.view
    o = 0
    xnc = V(o, [128, DC, T], BF16); o += DC * T * 2
    hpair = [V(o, [128, HMC, T], BF16), V(o + HMC * T * 2, [128, HMC, T], BF16)]
    o += 2 * HMC * T * 2
    combb = [V(o, [128, T], BF16), V(o + T * 2, [128, T], BF16)]; o += 2 * T * 2
    crow = V(o, [1, W], BF16); o += W * 2
    hs2 = [V(o, [128, W], BF16), V(o + W * 2, [128, W], BF16)]; o += 2 * W * 2
    gbuf = [V(o, [128, W], F32), V(o + W * 4, [128, W], F32)]; o += 2 * W * 4
    gout = [V(o, [128, W], F32), V(o + W * 4, [128, W], F32)]; o += 2 * W * 4
    assert o <= b.ARN, o
    ps_h = [b.ps[0].ap(), b.ps[1].ap()]
    ps_y = [b.ps[2].ap(), b.ps[3].ap()]
    ps_b = b.ps[4].ap()
    HW, ST, SV, SS = (b.SEM[x] for x in ("HW", "T", "V", "S"))
    C = b.consts
    sc = b.sched()
    chx = b.ch("b13x")
    for tc in range(NT):
        cols = slice(tc * W, (tc + 1) * W)
        sc.dma("sync", xnc[:, :, cols],
               b.scr_xn[:, cols].rearrange("(k p) s -> p k s", p=128), sem=chx)
    sc.wait("tensor", chx)
    sc.wait("scalar", chx)
    for p in range(NE // 2):
        es = [2 * p, 2 * p + 1]
        h_w1 = {e: b.wplan([b.hin["w1_til"][e][hc] for hc in range(HMC)]) for e in es}
        h_w2 = {e: [] for e in es}
        for k in range(DC):
            for e in es:
                h_w2[e].append(b.wplan([b.hin["w2_til"][e][k]], cin=HMC)[0])
        sc.wloads(h_w1[es[0]] + h_w1[es[1]] + [h_w2[e][k] for k in range(DC) for e in es])
        for j, e in enumerate(es):
            sc.wait("scalar", SV)
            for tc in range(NT):
                cols = slice(tc * W, (tc + 1) * W)
                sc.wait("tensor", SS)
                sc.do("tensor", lambda t, e=e, cols=cols: t.matmul(
                    ps_b[0:1, 0:W], C["ident_bf"].ap()[0:8, e:e + 1],
                    b.comb_fm.ap()[0:8, cols], start=True, stop=True))
                bst = sc.inc("tensor", ST)
                sc.wait("scalar", ST, bst)
                sc.do("scalar", lambda s: s.copy(crow, ps_b[0:1, 0:W]))
                cst = sc.inc("scalar", SS)
                sc.wait("tensor", SS, cst)
                sc.do("tensor", lambda t: t.matmul(
                    ps_b[:, 0:W], C["ones_row_bf"].ap(), crow, start=True, stop=True))
                bst2 = sc.inc("tensor", ST)
                sc.wait("scalar", ST, bst2)
                sc.do("scalar", lambda s, j=j, cols=cols: s.copy(combb[j][:, cols],
                                                                 ps_b[:, 0:W]))
                sc.inc("scalar", SS)
        hfree = [0, 0]
        hsfree = [0, 0]
        for j, e in enumerate(es):
            for hc in range(HMC):
                sc.wwait(h_w1[e][hc])
                for tc in range(NT):
                    i2 = (hc * NT + tc) % 2
                    if hfree[i2]:
                        sc.wait("tensor", SS, hfree[i2])
                    for k in range(DC):
                        sc.do("tensor", lambda t, hw=h_w1[e][hc], k=k, tc=tc, i2=i2:
                              t.matmul(ps_h[i2][:, 0:W], b.wview(hw, k),
                                       xnc[:, k, tc * W:(tc + 1) * W],
                                       start=(k == 0), stop=(k == DC - 1)))
                    hst = sc.inc("tensor", ST)
                    sc.wait("scalar", ST, hst)
                    if hsfree[i2]:
                        sc.wait("scalar", SV, hsfree[i2])
                    sc.do("scalar", lambda s, e=e, hc=hc, i2=i2: s.activation(
                        hs2[i2], ps_h[i2][:, 0:W], AF.Silu,
                        bias=C["b1_til"].ap()[:, e, hc:hc + 1]))
                    hss = hfree[i2] = sc.inc("scalar", SS)
                    sc.wait("vector", SS, hss)
                    sc.do("vector", lambda v, j=j, hc=hc, tc=tc, i2=i2,
                          cols=slice(tc * W, (tc + 1) * W):
                          v.tensor_mul(hpair[j][:, hc, cols], hs2[i2], combb[j][:, cols]))
                    if tc == NT - 1:
                        sc.inc("vector", b.SEM["WF"], 1)
                    hsfree[i2] = sc.inc("vector", SV)
        sc.wait("tensor", SV)
        yfree = [None, None]
        for k in range(DC):
            for e in es:
                sc.wwait(h_w2[e][k])
            for tc in range(NT):
                cols = slice(tc * W, (tc + 1) * W)
                i2 = (k * NT + tc) % 2
                if yfree[i2]:
                    sc.q["tensor"].append(("wait",) + yfree[i2])
                for j, e in enumerate(es):
                    for hc in range(HMC):
                        last_mm = (j == 1 and hc == HMC - 1 and p > 0)
                        sc.do("tensor", lambda t, hw=h_w2[e][k], hc=hc, j=j, tc=tc,
                              i2=i2, last_mm=last_mm:
                              t.matmul(ps_y[i2][:, 0:W], b.wview(hw, hc),
                                       hpair[j][:, hc, tc * W:(tc + 1) * W],
                                       start=(j == 0 and hc == 0), stop=last_mm))
                if p == 0:
                    sc.do("tensor", lambda t, k=k, cols=cols, i2=i2: t.matmul(
                        ps_y[i2][:, 0:W], C["b2"].ap()[:, k * 128:(k + 1) * 128],
                        b.comb_fm.ap()[:, cols], start=False, stop=True))
                yst = sc.inc("tensor", ST)
                chg = b.ch(f"b13g{i2}")
                if p > 0:
                    sc.wait("sync", SV)
                    sc.dma("sync", gbuf[i2], b.scr_g[k * 128:(k + 1) * 128, cols],
                           sem=chg)
                    sc.wait("vector", chg)
                    sc.wait("vector", ST, yst)
                    sc.wait("vector", b.ch(f"b13s{i2}"))
                    sc.drain("vector")
                    sc.do("vector", lambda v, i2=i2: v.tensor_add(gout[i2],
                                                                  ps_y[i2][:, 0:W],
                                                                  gbuf[i2]))
                    if tc == NT - 1:
                        sc.inc("vector", b.SEM["WF"], 1)
                        sc.inc("vector", b.SEM["WF"], 1)
                    yfree[i2] = (SV.h, sc.inc("vector", SV))
                else:
                    sc.wait("scalar", ST, yst)
                    sc.wait("scalar", b.ch(f"b13s{i2}"))
                    sc.do("scalar", lambda s, i2=i2: s.copy(gout[i2], ps_y[i2][:, 0:W]))
                    if tc == NT - 1:
                        sc.inc("scalar", b.SEM["WF"], 1)
                        sc.inc("scalar", b.SEM["WF"], 1)
                    yfree[i2] = (SS.h, sc.inc("scalar", SS))
                sc.wait("sync", SV if p > 0 else SS)
                sc.dma("sync", b.scr_g[k * 128:(k + 1) * 128, cols], gout[i2],
                       sem=b.ch(f"b13s{i2}"))
    b.run_phase("b13", sc, [("g", b.scr_g)])


def p_rope(b):
    cfg = b.cfg
    S = cfg.S
    NSC = S // 512 if S >= 512 else 1
    CW = min(512, S)
    V = b.view
    cosv, sinv = rope_region(b)
    o = 0
    jidx = V(o, [64, 1], F32); o += 32
    theta = V(o, [64, 1], F32); o += 32
    ang = V(o, [64, CW], F32); o += CW * 4
    ang2 = V(o, [64, CW], F32); o += CW * 4
    kf = V(o, [64, CW], F32); o += CW * 4
    ki = V(o, [64, CW], I32); o += CW * 4
    kb = V(o, [64, CW], F32); o += CW * 4
    red = V(o, [64, CW], F32); o += CW * 4
    cen = V(o, [64, CW], F32); o += CW * 4
    sv = V(o, [64, CW], F32); o += CW * 4
    posr = V(o, [1, S], F32); o += S * 4
    ps_p = b.ps[0].ap()
    ST, SV, SS, SG = (b.SEM[x] for x in ("T", "V", "S", "G"))
    C = b.consts
    PI = math.pi
    sc = b.sched()
    sc.dma("sync", posr, b.hin["pos_rot"], sem=b.ch("rp"))
    sc.q["gpsimd"].append(("wait", b.SEM["BAR"].h, 4 * b.phase_idx))
    sc.do("gpsimd", lambda g: g.iota(jidx, [[0, 1]], channel_multiplier=1,
                                     allow_small_or_imprecise_dtypes=True))
    sc.inc("gpsimd", SG)
    sc.wait("scalar", SG)
    sc.do("scalar", lambda s: s.activation(theta, jidx, AF.Exp,
                                           scale=-math.log(10000.0) / 64.0))
    sc.inc("scalar", SS)

    for scn in range(max(1, S // CW)):
        cols = slice(scn * CW, (scn + 1) * CW)
        sc.wait("tensor", SS)
        sc.wait("tensor", SV)
        if scn == 0:
            sc.wait("tensor", b.ch("rp"))
        sc.do("tensor", lambda t, cols=cols: t.matmul(
            ps_p[0:64, 0:CW], C["ones_row_f"].ap()[:, 0:64],
            posr[:, cols], start=True, stop=True))
        sc.inc("tensor", ST)
        sc.wait("vector", ST)
        sc.wait("vector", SS)
        sc.drain("vector")
        sc.do("vector", lambda v: v.tensor_scalar(ang, ps_p[0:64, 0:CW], theta, None,
                                                  ALU.mult))
        sc.inc("vector", SV)
        for which, dst in (("sin", sinv), ("cos", cosv)):
            a = ang
            if which == "cos":
                sc.drain("vector")
                sc.do("vector", lambda v: v.tensor_scalar_add(ang2, ang, PI / 2.0))
                a = ang2
            sc.drain("vector")
            sc.do("vector", lambda v, a=a: v.tensor_scalar_mul(kf, a, 1.0 / LN2PI))
            sc.drain("vector")
            sc.do("vector", lambda v: v.tensor_copy(ki, kf))
            sc.drain("vector")
            sc.do("vector", lambda v: v.tensor_copy(kb, ki))
            sc.drain("vector")
            sc.do("vector", lambda v, a=a: v.scalar_tensor_tensor(
                red, kb, -LN2PI, a, ALU.mult, ALU.add))
            sc.drain("vector")
            sc.do("vector", lambda v: v.tensor_scalar_add(cen, red, -PI))
            sc.drain("vector")
            sc.do("vector", lambda v: v.tensor_scalar_max(sv, cen, -PI))
            sc.drain("vector")
            sc.do("vector", lambda v: v.tensor_scalar_min(cen, sv, PI))
            sc.inc("vector", SV)
            sc.wait("scalar", SV)
            sc.do("scalar", lambda s: s.activation(sv, cen, AF.Sin))
            sc.inc("scalar", SS)
            sc.wait("vector", SS)
            sc.drain("vector")
            sc.do("vector", lambda v, dst=dst, cols=cols: v.tensor_scalar_mul(
                dst[:, cols], sv, -1.0))
            sc.inc("vector", SV)
    b.run_phase("rope", sc, [("cos", cosv), ("sin", sinv)])


def p_b16(b):
    cfg = b.cfg
    T, W = cfg.T, cfg.SC
    NT = T // W
    V = b.view
    o = DC * cfg.S * 2
    o = 0
    sbuf_s = V(o, [128, W], F32); o += W * 4
    gbuf = [V(o, [128, W], F32), V(o + W * 4, [128, W], F32)]; o += 2 * W * 4
    gs = V(o, [128, W], F32); o += W * 4
    xo = [V(o, [128, W], F32), V(o + W * 4, [128, W], F32)]; o += 2 * W * 4
    prod = V(o, [128, W], F32); o += W * 4
    xnew = [V(o, [128, W], F32), V(o + W * 4, [128, W], F32)]; o += 2 * W * 4
    ps_s = [b.ps[2].ap(), b.ps[3].ap()]
    HW, ST, SV, SS = (b.SEM[x] for x in ("HW", "T", "V", "S"))
    C = b.consts
    sc = b.sched()
    hwo = b.wplan([b.hin["l1wo_til"][k] for k in range(DC)])
    sc.wloads(hwo)
    for k in range(DC):
        sc.wwait(hwo[k])
        for tc in range(NT):
            cols = slice(tc * W, (tc + 1) * W)
            i2 = (k * NT + tc) % 2
            sc.wait("tensor", SS)
            for hc in range(DC):
                sc.do("tensor", lambda t, k=k, hc=hc, tc=tc, i2=i2, cols=cols: t.matmul(
                    ps_s[i2][:, 0:W], b.wview(hwo[k], hc),
                    b.ctx_all[:, hc, cols], start=(hc == 0), stop=(hc == DC - 1)))
            if tc == NT - 1:
                sc.inc("tensor", b.SEM["WF"], 1)
            mmk = sc.inc("tensor", ST)
            sc.wait("sync", SV)
            sc.wait("sync", b.ch(f"b16s{i2}"))
            chg8 = b.ch(f"b16g{i2}")
            cho8 = b.ch(f"b16o{i2}")
            sc.dma("sync", gbuf[i2], b.scr_g[k * 128:(k + 1) * 128, cols], sem=chg8)
            sc.dma("sync", xo[i2], b.cc_in.ap()[k * 128:(k + 1) * 128, cols], sem=cho8)
            sc.wait("scalar", ST, mmk)
            sc.wait("scalar", chg8)
            sc.drain("scalar")
            sc.do("scalar", lambda s, i2=i2: s.copy(sbuf_s, ps_s[i2][:, 0:W]))
            sc.drain("scalar")
            sc.do("scalar", lambda s, i2=i2: s.activation(gs, gbuf[i2], AF.Silu))
            sc.inc("scalar", SS)
            sc.wait("vector", SS)
            sc.wait("vector", cho8)
            sc.wait("vector", b.ch(f"b16s{i2}"))
            sc.drain("vector")
            sc.do("vector", lambda v, i2=i2: v.tensor_mul(xnew[i2], sbuf_s, gs))
            sc.drain("vector")
            sc.do("vector", lambda v, i2=i2: v.tensor_add(gbuf[i2], xnew[i2], xo[i2]))
            sc.inc("vector", SV)
            sc.wait("sync", SV)
            sc.dma("sync", b.y_ext[k * 128:(k + 1) * 128, cols], gbuf[i2],
                   sem=b.ch(f"b16s{i2}"))
    b.run_phase("b16", sc, [])


def p_loadxn(b):
    cfg = b.cfg
    S, SC = cfg.S, cfg.SC
    XN1 = xn1_region(b)
    sc = b.sched()
    chx = b.ch("lxn")
    for scn in range(S // SC):
        cols = slice(scn * SC, (scn + 1) * SC)
        sc.dma("sync", XN1[:, :, cols],
               b.scr_xn[:, cols].rearrange("(k p) s -> p k s", p=128), sem=chx)
    sc.wait("sync", chx)
    b.run_phase("lxn", sc)



_BUILD_CACHE = {}


def _get_program(mode, group=None):
    key = (mode, group)
    if key not in _BUILD_CACHE:
        cfg = Cfg(ncores=8, S=2048)
        _BUILD_CACHE[key] = (cfg, build_program(cfg, mode=mode, group=group))
    return _BUILD_CACHE[key]


def kernel(**inputs):
    cfg, b = _get_program("fused")
    shared = host_layout(inputs, cfg)
    x = np.asarray(inputs["L0_attn_x"], dtype=np.float32)
    declared = set(b.hin.keys())
    in_maps = []
    for c in range(cfg.ncores):
        m = per_core_inputs(x, shared, cfg, c)
        in_maps.append({k: v for k, v in m.items() if k in declared})
    res = run_bass_kernel_spmd(b.nc, in_maps, core_ids=list(range(cfg.ncores)))
    out = np.empty((4, 2048, 2048), dtype=np.float32)
    T = cfg.T
    for c in range(cfg.ncores):
        bq, half = c // 2, c % 2
        out[bq, half * T:(half + 1) * T, :] = res.results[c]["y_ext"].T
    return out


# version 08 (final)
# speedup vs baseline: 1.0529x; 1.0529x over previous
import math
from contextlib import ExitStack

import numpy as np
import concourse.bass as bass
import concourse.mybir as mybir
from concourse.bass_utils import run_bass_kernel_spmd

F32 = mybir.dt.float32
BF16 = mybir.dt.bfloat16
I32 = mybir.dt.int32
U8 = mybir.dt.uint8
AF = mybir.ActivationFunctionType
ALU = mybir.AluOpType
AX = mybir.AxisListType.X

D = 2048
DC = 16
NH = 16
DH = 128
FF = 8192
FFC = 64
HM = 1408
HMC = 11
NE = 8
NEG = -1.0e9
CSHIFT = -20.0
LN2PI = 2.0 * math.pi


class Cfg:
    def __init__(self, ncores=8, S=2048, debug=()):
        self.ncores = ncores
        self.S = S
        self.T = S // 2
        self.SC = min(512, self.T)
        self.TCH = min(512, self.T)
        self.NSC = self.S // self.SC
        self.NTC = self.T // self.TCH
        self.KT = self.S // 128
        self.debug = set(debug)
        self.RING = 8



def _pretile(wT):
    Din, Dout = wT.shape
    ci, co = Din // 128, Dout // 128
    return np.ascontiguousarray(
        wT.reshape(ci, 128, co, 128).transpose(2, 1, 0, 3))


def _deinterleave_cols(wT):
    Din, Dout = wT.shape
    w = wT.reshape(Din, NH, DH)
    perm = np.concatenate([np.arange(0, DH, 2), np.arange(1, DH, 2)])
    return np.ascontiguousarray(w[:, :, perm].reshape(Din, Dout))


def _vec_T(v):
    return np.ascontiguousarray(v.reshape(-1, 128).T)


def host_layout(inputs, cfg):
    f = np.float32
    out = {}
    out["wv0_til"] = _pretile(np.ascontiguousarray(
        inputs["L0_attn_Wqkv"][:, :, 256:384].reshape(D, D)).astype(f, copy=False))
    out["bv0_T"] = np.ascontiguousarray(
        inputs["L0_attn_bqkv"][:, 256:384].astype(f).T)
    out["wout_til"] = _pretile(inputs["L0_attn_Wout"].reshape(D, D).astype(f, copy=False))
    out["bout_T"] = _vec_T(inputs["L0_attn_bout"].astype(f))
    out["sbias_T"] = _vec_T(inputs["L0_ffn_simple_bias"].astype(f))
    out["router_til"] = np.ascontiguousarray(
        inputs["L0_ffn_w_router"].astype(f).T.reshape(DC, 128, 3))
    out["wdown_til"] = np.ascontiguousarray(
        inputs["L0_ffn_w_down"].astype(f).T.reshape(DC, 128, 32))
    out["wup_T"] = np.ascontiguousarray(inputs["L0_ffn_w_up"].astype(f).T)
    for nm in ("ln1_scale", "ln1_bias", "ln2_scale", "ln2_bias"):
        out[nm] = _vec_T(inputs[f"L0_ffn_{nm}"].astype(f))
    for nm in ("wq", "wk", "wv", "wo"):
        out[f"l0{nm}_til"] = _pretile(np.ascontiguousarray(
            inputs[f"L0_ffn_{nm}"].astype(f).T))
    out["ff1_til"] = _pretile(np.ascontiguousarray(inputs["L0_ffn_w_ff1"].astype(f).T))
    out["ff2_til"] = _pretile(np.ascontiguousarray(inputs["L0_ffn_w_ff2"].astype(f).T))
    for nm in ("Wq", "Wk"):
        out[f"l1{nm.lower()}_til"] = _pretile(_deinterleave_cols(
            np.ascontiguousarray(inputs[f"L1_attn_{nm}"].astype(f).T)))
    for nm in ("Wv", "Wo"):
        out[f"l1{nm.lower()}_til"] = _pretile(np.ascontiguousarray(
            inputs[f"L1_attn_{nm}"].astype(f).T))
    out["gate_til"] = np.ascontiguousarray(
        inputs["L1_ffn_gate_w"].astype(f).T.reshape(DC, 128, NE))
    out["gate_b"] = np.ascontiguousarray(inputs["L1_ffn_gate_b"].astype(f).reshape(NE, 1))
    out["bias_e"] = np.ascontiguousarray(inputs["L1_ffn_bias"].astype(f).reshape(NE, 1))
    out["w1_til"] = np.ascontiguousarray(np.stack(
        [_pretile(np.ascontiguousarray(inputs["L1_ffn_w1"][e].astype(f).T))
         for e in range(NE)]))
    out["b1_til"] = np.ascontiguousarray(
        inputs["L1_ffn_b1"].astype(f).reshape(NE, HMC, 128).transpose(0, 2, 1))
    out["w2_til"] = np.ascontiguousarray(np.stack(
        [_pretile(np.ascontiguousarray(inputs["L1_ffn_w2"][e].astype(f).T))
         for e in range(NE)]))
    out["b2"] = np.ascontiguousarray(inputs["L1_ffn_b2"].astype(f))

    import ml_dtypes
    out["ident_bf"] = np.eye(128, dtype=ml_dtypes.bfloat16)
    out["ident_f32"] = np.eye(128, dtype=f)
    SC = cfg.SC
    i = np.arange(128)[:, None]
    j = np.arange(SC)[None, :]
    out["maskband"] = np.ascontiguousarray(np.stack(
        [np.where(i + o <= j, 0.0, NEG).astype(f) for o in range(0, SC, 128)]))
    return out


def shard_x(x, cfg, core):
    b, half = core // 2, core % 2
    return np.ascontiguousarray(x[b].T)



class Sem:
    __slots__ = ("h", "n", "name")

    def __init__(self, h, name):
        self.h = h
        self.n = 0
        self.name = name


class WChunk:
    def __init__(self, idx, slot, cin, null=False, quad=False):
        self.idx = idx
        self.slot = slot
        self.cin = cin
        self.null = null
        self.quad = quad
        self.ap = None
        self.rank = 0


class Builder:
    def __init__(self, cfg, mode="fused", group=None):
        self.cfg = cfg
        self.mode = mode
        self.group = group
        self.nc = bass.Bass("TRN2", target_bir_lowering=False)
        self.es = ExitStack()
        self.phase_idx = 0
        self.barrier_engines = 4
        self._dbg = {}
        self._dump_pending = []
        self._mkio()
        self._mksems()
        self._mksbuf()
        self._wchunks = 0
        self._wloads = 0

    def _mkio(self):
        nc, cfg = self.nc, self.cfg
        S, T = cfg.S, cfg.T

        def di(name, shape, dtype=F32):
            return nc.dram_tensor(name, list(shape), dtype, kind="ExternalInput")

        g = self.group
        self.hin = {}
        names = {
            "xseq": (D, S), "wv0_til": (16, 128, 16, 128), "bv0_T": (128, 16),
            "wout_til": (16, 128, 16, 128), "bout_T": (128, 16),
            "sbias_T": (128, 16), "router_til": (16, 128, 3), "wdown_til": (16, 128, 32),
            "wup_T": (32, 2048),
            "ln1_scale": (128, 16), "ln1_bias": (128, 16),
            "ln2_scale": (128, 16), "ln2_bias": (128, 16),
            "l0wq_til": (16, 128, 16, 128), "l0wk_til": (16, 128, 16, 128),
            "l0wv_til": (16, 128, 16, 128), "l0wo_til": (16, 128, 16, 128),
            "ff1_til": (64, 128, 16, 128), "ff2_til": (16, 128, 64, 128),
            "l1wq_til": (16, 128, 16, 128), "l1wk_til": (16, 128, 16, 128),
            "l1wv_til": (16, 128, 16, 128), "l1wo_til": (16, 128, 16, 128),
            "gate_til": (16, 128, 8), "gate_b": (8, 1), "bias_e": (8, 1),
            "w1_til": (8, 11, 128, 16, 128), "b1_til": (8, 128, 11),
            "w2_til": (8, 16, 128, 11, 128), "b2": (8, 2048),
            "ident_bf": (128, 128), "ident_f32": (128, 128),
            "maskband": (cfg.SC // 128, 128, cfg.SC),
            "pmask": (128, 1), "pos_rot": (1, S),
        }
        g0_only = {"xseq", "wv0_til", "bv0_T", "wout_til", "bout_T", "sbias_T",
                   "router_til", "wdown_til", "wup_T", "ln1_scale", "ln1_bias",
                   "ln2_scale", "ln2_bias", "l0wq_til", "l0wk_til", "l0wv_til",
                   "l0wo_til", "ff1_til", "ff2_til"}
        g1_only = {"l1wq_til", "l1wk_til", "l1wv_til", "l1wo_til", "gate_til",
                   "gate_b", "bias_e", "w1_til", "b1_til", "w2_til", "b2"}
        for nm, sh in names.items():
            if g == 0 and nm in g1_only:
                continue
            if g == 1 and nm in g0_only:
                continue
            dt = BF16 if nm == "ident_bf" else F32
            self.hin[nm] = di(nm, sh, dt).ap()

        self.scr_xn = nc.dram_tensor("scr_xn", [D, S], BF16).ap()
        self.scr_g = nc.dram_tensor("scr_g", [D, T], F32).ap()
        self.scr_y = nc.dram_tensor("scr_y", [D, T], F32).ap()
        self.scr_f = nc.dram_tensor("scr_f", [D, T], F32).ap()
        if self.mode == "fused":
            self.cc_in = nc.dram_tensor("cc_in", [D, T], F32)
            self.cc_out = nc.dram_tensor("cc_out", [D, T], F32)
        elif g == 0:
            self.x1_out = nc.dram_tensor("x1_out", [D, T], F32, kind="ExternalOutput")
            self.cc_in = self.x1_out
        else:
            self.cc_in = nc.dram_tensor("cc_in_x", [D, T], F32, kind="ExternalInput")
            self.cc_out = nc.dram_tensor("cc_sum", [D, T], F32, kind="ExternalInput")
        if self.mode == "fused" or g == 1:
            self.y_ext = nc.dram_tensor("y_ext", [D, T], F32, kind="ExternalOutput").ap()

    def dbg(self, name, shape, dtype=F32):
        t = self.nc.dram_tensor("dbg_" + name, list(shape), dtype, kind="ExternalOutput")
        self._dbg[name] = t
        return t.ap()

    def _mksems(self):
        nc = self.nc
        self.SEM = {}
        for nm in ("BAR", "HW", "HWS", "WL", "WF", "T", "V", "S", "G", "CC", "GD", "HA", "HB"):
            self.SEM[nm] = Sem(self.es.enter_context(nc.semaphore("sem_" + nm)), nm)
        self.WLS = [Sem(self.es.enter_context(nc.semaphore(f"sem_wl{i}")), f"wl{i}")
                    for i in range(self.cfg.RING)]
        self._wl_uses = [0] * self.cfg.RING
        self.chans = {}
        self.dma_sems = [self.SEM["HW"], self.SEM["HWS"], self.SEM["HA"], self.SEM["HB"]]

    def ch(self, name):
        if name not in self.chans:
            s = Sem(self.es.enter_context(self.nc.semaphore("ch_" + name)), name)
            self.chans[name] = s
            self.dma_sems.append(s)
        return self.chans[name]

    def inc(self, instr, sem, k=1):
        sem.n += k
        instr.then_inc(sem.h, k)
        return sem.n

    def wait(self, eng, sem, val=None):
        eng.wait_ge(sem.h, sem.n if val is None else val)

    def _mksbuf(self):
        nc, cfg = self.nc, self.cfg
        A = nc.alloc_sbuf_tensor
        S, T = cfg.S, cfg.T
        self.wring = A("wring", [128, cfg.RING, 16, 128], BF16)
        self.ctx_all = A("ctx_all", [128, DC, T], BF16)
        self.consts = {}
        for nm, sh, dt in (
            ("ident_bf", [128, 128], BF16), ("ident_f32", [128, 128], F32),
            ("maskband", [128, cfg.SC // 128, cfg.SC], F32),
            ("bv0_T", [128, 16], F32), ("bout_T", [128, 16], F32),
            ("sbias_T", [128, 16], F32),
            ("ln1_scale", [128, 16], F32), ("ln1_bias", [128, 16], F32),
            ("ln2_scale", [128, 16], F32), ("ln2_bias", [128, 16], F32),
            ("router_til", [128, 16, 3], F32), ("wdown_til", [128, 16, 32], BF16),
            ("wup_T", [32, 2048], BF16), ("gate_til", [128, 16, 8], F32),
            ("gate_b", [8, 1], F32), ("bias_e", [8, 1], F32),
            ("b1_til", [128, 8, 11], F32), ("b2", [8, 2048], BF16),
            ("ones_col_bf", [128, 1], BF16), ("ones_row_f", [1, 128], F32),
            ("ones_row_bf", [1, 128], BF16),
            ("bvsum", [128, 1], F32), ("gbsum", [8, 1], F32),
            ("pmask", [128, 1], F32), ("pm_c", [128, 1], F32),
            ("cshift", [128, 1], F32),
        ):
            self.consts[nm] = A("c_" + nm, sh, dt)
        self.lg_fm = A("lg_fm", [8, T], F32)
        self.comb_fm = A("comb_fm", [8, T], BF16)
        self.sel2f = A("sel2f", [128, T], U8)
        self.ARN = (nc.sbuf_bytes_remaining - 512) // 32 * 32
        self.AR = A("arena", [128, self.ARN], U8)
        self.ps = [nc.alloc_psum_tensor(f"ps{i}", [128, 512], F32) for i in range(8)]
        rem = nc.sbuf_bytes_remaining
        assert rem >= 0, f"SBUF over budget by {-rem}"

    def view(self, off_bytes, shape, dtype, arena=None):
        arena = arena or self.AR
        n = int(np.prod(shape[1:])) * mybir.dt.size(dtype)
        assert off_bytes + n <= self.ARN, (off_bytes, n)
        ap = arena.ap()[0:shape[0], off_bytes:off_bytes + n].bitcast(dtype)
        if len(shape) == 2:
            return ap
        assert len(shape) == 3
        return ap.rearrange("p (a b) -> p a b", b=shape[2])

    def wplan(self, aps, cin=16, quad=False):
        out = []
        if quad:
            assert len(aps) == 4 and self.cfg.RING % 4 == 0
            while self._wchunks % 4 != 0:
                h = WChunk(self._wchunks, self._wchunks % self.cfg.RING, 0, null=True)
                h.ap = None
                h.rank = self._wloads
                self._wchunks += 1
                out.append(h)
        for ap in aps:
            h = WChunk(self._wchunks, self._wchunks % self.cfg.RING, cin, quad=quad)
            h.ap = ap
            self._wl_uses[h.slot] += 1
            h.rank = self._wl_uses[h.slot]
            self._wchunks += 1
            out.append(h)
        return out

    def emit_wloads(self, g, handles):
        WL, WF = self.SEM["WL"], self.SEM["WF"]
        for h in handles:
            if h.idx >= self.cfg.RING:
                g.wait_ge(WF.h, h.idx - self.cfg.RING + 1)
            if h.null:
                continue
            ins = g.dma_start(out=self.wring[:, h.slot, 0:h.cin, :], in_=h.ap)
            WL.n += 16
            ins.then_inc(WL.h, 16)

    def wwait(self, te, h):
        if not h.null:
            te.wait_ge(self.SEM["WL"].h, 16 * h.rank)

    def wview(self, h, k=None, quad=False):
        if quad:
            return self.wring[:, h.slot:h.slot + 4, k, :]
        if k is None:
            return self.wring[:, h.slot, 0:h.cin, :]
        return self.wring[:, h.slot, k, :]

    def wrel(self, instr, hs):
        if not isinstance(hs, (list, tuple)):
            hs = [hs]
        self.inc(instr, self.SEM["WF"], len(hs))

    def sched(self):
        return Sched(self)

    def run_phase(self, name, sc, dumps=(), final=False):
        b = self
        bar = b.SEM["BAR"]
        k = b.phase_idx
        with b.nc.Block(name) as blk:
            for engname in ("tensor", "vector", "scalar", "sync"):
                ops = sc.q[engname]

                def wrapped(eng, ops=ops, engname=engname):
                    if k > 0:
                        eng.wait_ge(bar.h, 4 * k)
                    if engname == "sync" and final:
                        hw = b.SEM["HW"]
                        if b._dump_pending:
                            eng.wait_ge(hw.h, hw.n)
                            eng.wait_ge(b.SEM["HWS"].h, b.SEM["HWS"].n)
                        for ap_dst, ap_src in b._dump_pending:
                            ins = eng.dma_start(out=ap_dst, in_=ap_src)
                            hw.n += 16
                            ins.then_inc(hw.h, 16)
                        b._dump_pending = []
                    last = None
                    for op in ops:
                        kind = op[0]
                        if kind == "do":
                            last = op[1](eng)
                        elif kind == "wait":
                            eng.wait_ge(op[1], op[2])
                        elif kind == "inc":
                            assert last is not None, (name, engname, "inc w/o instr")
                            last.then_inc(op[1], op[2])
                        elif kind == "drain":
                            eng.drain()
                    if engname == "sync":
                        for s_ in b.dma_sems:
                            if s_.n:
                                eng.wait_ge(s_.h, s_.n)
                    if final and engname == "sync":
                        eng.wait_ge(b.SEM["GD"].h, b.SEM["GD"].n)
                        for s_ in b.WLS:
                            if s_.n:
                                eng.wait_ge(s_.h, s_.n)
                    ins = eng.drain()
                    ins.then_inc(bar.h, 1)
                getattr(blk, engname)(wrapped)
                bar.n += 1
            if sc.q["gpsimd"]:
                def gwrapped(g, ops=sc.q["gpsimd"]):
                    last = None
                    for op in ops:
                        kind = op[0]
                        if kind == "do":
                            last = op[1](g)
                        elif kind == "wait":
                            g.wait_ge(op[1], op[2])
                        elif kind == "inc":
                            last.then_inc(op[1], op[2])
                blk.gpsimd(gwrapped)
        b.phase_idx += 1
        for nm, ap in dumps:
            if name in b.cfg.debug or "*" in b.cfg.debug:
                dst = b.dbg(f"{name}_{nm}", list(ap.shape),
                            ap.dtype)
                b._dump_pending.append((dst, ap))


class Sched:
    def __init__(self, b):
        self.b = b
        self.q = {"tensor": [], "vector": [], "scalar": [], "sync": [], "gpsimd": []}

    def do(self, eng, fn):
        self.q[eng].append(("do", fn))

    def wait(self, eng, sem, val=None):
        self.q[eng].append(("wait", sem.h, sem.n if val is None else val))

    def inc(self, eng, sem, k=1):
        sem.n += k
        if self.q[eng] and self.q[eng][-1][0] != "do":
            self.q[eng].append(("do", lambda e: e.drain()))
        self.q[eng].append(("inc", sem.h, k))
        return sem.n

    def drain(self, eng):
        self.q[eng].append(("drain",))

    def dma(self, eng, out, in_, sem=None):
        self.do(eng, lambda e, out=out, in_=in_: e.dma_start(out=out, in_=in_))
        sem = sem or self.b.SEM["HW"]
        return self.inc(eng, sem, 16)

    def wloads(self, handles):
        b = self.b
        WF = b.SEM["WF"]
        for h in handles:
            if h.idx >= b.cfg.RING:
                self.q["gpsimd"].append(("wait", WF.h, h.idx - b.cfg.RING + 1))
            if h.null:
                continue
            self.do("gpsimd", lambda g, h=h: g.dma_start(
                out=b.wring[:, h.slot, 0:h.cin, :], in_=h.ap))
            self.inc("gpsimd", b.WLS[h.slot], 16)

    def wwait(self, h):
        if not h.null:
            self.q["tensor"].append(("wait", self.b.WLS[h.slot].h, 16 * h.rank))



def p_consts(b):
    sc = b.sched()
    C = b.consts
    HW, SV = b.SEM["HW"], b.SEM["V"]
    for nm in ("ident_bf", "ident_f32", "maskband", "bv0_T", "bout_T",
               "sbias_T", "ln1_scale", "ln1_bias", "ln2_scale", "ln2_bias",
               "router_til", "wdown_til", "wup_T", "gate_til", "gate_b",
               "bias_e", "b1_til", "b2", "pmask"):
        if nm not in b.hin:
            continue
        src = b.hin[nm]
        if nm == "maskband":
            src = src.rearrange("a p s -> p a s")
        if nm == "b1_til":
            src = src.rearrange("e p h -> p e h")
        if nm in ("router_til", "wdown_til", "gate_til"):
            src = src.rearrange("k p r -> p k r")
        if nm in ("wup_T", "b2", "wdown_til"):
            sc.dma("gpsimd", C[nm].ap(), src, sem=b.SEM["GD"])
        else:
            sc.dma("sync", C[nm].ap(), src)
    sc.wait("vector", HW)
    sc.wait("vector", b.SEM["GD"])
    sc.do("vector", lambda v: v.memset(C["ones_col_bf"].ap(), 1.0))
    sc.do("vector", lambda v: v.memset(C["ones_row_f"].ap(), 1.0))
    sc.do("vector", lambda v: v.memset(C["ones_row_bf"].ap(), 1.0))
    sc.do("vector", lambda v: v.memset(C["cshift"].ap(), CSHIFT))
    if "bv0_T" in b.hin:
        sc.do("vector", lambda v: v.reduce_sum(C["bvsum"].ap(), C["bv0_T"].ap(), axis=AX))
    if "gate_b" in b.hin:
        sc.do("vector", lambda v: v.tensor_add(C["gbsum"].ap(), C["gate_b"].ap(),
                                               C["bias_e"].ap()))
    if "pmask" in b.hin:
        sc.do("vector", lambda v: v.tensor_scalar_add(C["pm_c"].ap(), C["pmask"].ap(),
                                                      CSHIFT))
    sc.inc("vector", SV)
    b.run_phase("consts", sc)


def p_ln_outer(b, block):
    cfg = b.cfg
    S, T = cfg.S, cfg.T
    SC = cfg.SC
    NSC = S // SC
    NOC = T // SC
    name = f"b{block}_ln"
    V = b.view
    o = 0
    xin = V(o, [128, DC, SC], F32); o += DC * SC * 4
    ob = None
    if block == 1:
        ob = [V(o, [128, SC], F32), V(o + SC * 4, [128, SC], F32)]
        o += 2 * SC * 4
    xbf = V(o, [128, DC, SC], BF16); o += DC * SC * 2
    sqbf = V(o, [128, DC, SC], BF16); o += DC * SC * 2
    xnbf = V(o, [128, DC, SC], BF16); o += DC * SC * 2
    xnA = [V(o, [128, SC], F32), V(o + SC * 4, [128, SC], F32)]; o += 2 * SC * 4
    xnB = [V(o, [128, SC], F32), V(o + SC * 4, [128, SC], F32)]; o += 2 * SC * 4
    mu = V(o, [1, SC], F32); o += SC * 4
    musq = V(o, [1, SC], F32); o += SC * 4
    var = V(o, [1, SC], F32); o += SC * 4
    lnv = V(o, [1, SC], F32); o += SC * 4
    rstd = V(o, [1, SC], F32); o += SC * 4
    ps_sum, ps_sq, ps_mu, ps_rstd, ps_lg = (p.ap()[:, 0:SC] for p in b.ps[:5])
    nlg = 3 if block == 0 else 8
    lg_til = b.consts["router_til" if block == 0 else "gate_til"].ap()
    HW, ST, SV, SS = b.SEM["HW"], b.SEM["T"], b.SEM["V"], b.SEM["S"]
    sc = b.sched()
    for scn in range(NSC):
        own = scn < NOC
        xsrc = xin if (block == 0 or own or b.mode != "fused") else xbf
        sc.wait("sync", SV)
        sc.wait("sync", SS)
        if "b1dbg" in b.cfg.debug:
            sc.wait("sync", b.ch("dbg1"))
        diffmode = (block == 1 and not own and b.mode == "fused")
        if block == 0:
            src = b.hin["xseq"][:, scn * SC:(scn + 1) * SC].rearrange(
                "(k p) s -> p k s", p=128)
            sc.dma("sync", xin, src)
        else:
            if scn == 0 and b.mode == "fused":
                sc.wait("sync", b.SEM["CC"])
            cols = slice((scn % NOC) * SC, (scn % NOC + 1) * SC)
            if not own:
                srcb = b.cc_out if b.mode == "fused" else b.cc_in
                sc.dma("sync", xin, srcb.ap()[:, cols].rearrange(
                    "(k p) s -> p k s", p=128))
            else:
                sc.dma("sync", xin, b.cc_in.ap()[:, cols].rearrange(
                    "(k p) s -> p k s", p=128))
        ld_hw = HW.n
        if diffmode:
            obfree = [0, 0]
            cols = slice((scn % NOC) * SC, (scn % NOC + 1) * SC)
            for k in range(DC):
                cho = b.ch(f"lnob{k % 2}")
                if obfree[k % 2]:
                    sc.wait("sync", SV, obfree[k % 2])
                sc.dma("sync", ob[k % 2],
                       b.cc_in.ap()[k * 128:(k + 1) * 128, cols], sem=cho)
                if k == 0:
                    sc.wait("vector", HW, ld_hw)
                sc.wait("vector", cho)
                sc.drain("vector")
                sc.do("vector", lambda v, k=k: v.tensor_sub(xbf[:, k, :], xin[:, k, :],
                                                            ob[k % 2]))
                obfree[k % 2] = sc.inc("vector", SV)
            sc.wait("scalar", SV)
            sc.wait("scalar", b.ch(f"ln{block}s"))
            for k in range(DC):
                sc.do("scalar", lambda s, k=k: s.activation(
                    sqbf[:, k, :], xbf[:, k, :], AF.Square))
                sc.inc("scalar", SS)
        else:
            sc.wait("scalar", HW, ld_hw)
            sc.wait("scalar", b.ch(f"ln{block}s"))
            for k in range(DC):
                sc.do("scalar", lambda s, k=k, xsrc=xsrc: s.copy(xbf[:, k, :],
                                                                 xsrc[:, k, :]))
                sc.do("scalar", lambda s, k=k, xsrc=xsrc: s.activation(
                    sqbf[:, k, :], xsrc[:, k, :], AF.Square))
                sc.inc("scalar", SS)
        for k in range(DC):
            sc.wait("tensor", SS, SS.n - (DC - 1 - k))
            sc.do("tensor", lambda t, k=k: t.matmul(
                ps_sum[0:1, :], b.consts["ones_col_bf"].ap(), xbf[:, k, :],
                start=(k == 0), stop=(k == DC - 1)))
            sc.do("tensor", lambda t, k=k: t.matmul(
                ps_sq[0:1, :], b.consts["ones_col_bf"].ap(), sqbf[:, k, :],
                start=(k == 0), stop=(k == DC - 1)))
        sc.inc("tensor", ST)
        sc.wait("scalar", ST)
        sc.do("scalar", lambda s: s.activation(mu, ps_sum[0:1, :], AF.Copy, scale=1.0 / D))
        sc.inc("scalar", SS)
        sc.wait("vector", SS)
        sc.do("vector", lambda v: v.tensor_mul(musq, mu, mu))
        sc.drain("vector")
        sc.do("vector", lambda v: v.tensor_scalar_sub(musq, musq, 1e-5))
        sc.drain("vector")
        sc.do("vector", lambda v: v.scalar_tensor_tensor(
            var, ps_sq[0:1, :], 1.0 / D, musq, ALU.mult, ALU.subtract))
        sc.inc("vector", SV)
        sc.wait("scalar", SV)
        sc.do("scalar", lambda s: s.activation(lnv, var, AF.Ln))
        sc.drain("scalar")
        sc.do("scalar", lambda s: s.activation(rstd, lnv, AF.Exp, scale=-0.5))
        sc.inc("scalar", SS)
        sc.wait("tensor", SS)
        sc.do("tensor", lambda t: t.matmul(ps_mu, b.consts["ones_row_f"].ap(), mu,
                                           start=True, stop=True))
        sc.do("tensor", lambda t: t.matmul(ps_rstd, b.consts["ones_row_f"].ap(), rstd,
                                           start=True, stop=True))
        sc.inc("tensor", ST)
        sc.wait("vector", ST)
        ss_k = [0] * DC
        st_k = [0] * DC
        for k in range(DC):
            if k >= 2:
                sc.wait("vector", SS, ss_k[k - 2])
                if own:
                    sc.wait("vector", ST, st_k[k - 2])
            sc.do("vector", lambda v, k=k, xsrc=xsrc: v.tensor_sub(
                xnA[k % 2], xsrc[:, k, :], ps_mu))
            sc.drain("vector")
            sc.do("vector", lambda v, k=k: v.tensor_mul(xnB[k % 2], xnA[k % 2], ps_rstd))
            svk = sc.inc("vector", SV)
            sc.wait("scalar", SV, svk)
            sc.do("scalar", lambda s, k=k: s.copy(xnbf[:, k, :], xnB[k % 2]))
            ss_k[k] = sc.inc("scalar", SS)
            if own:
                sc.wait("tensor", SV, svk)
                sc.do("tensor", lambda t, k=k: t.matmul(
                    ps_lg[0:nlg, :], lg_til[:, k, 0:nlg], xnB[k % 2],
                    start=(k == 0), stop=(k == DC - 1)))
                st_k[k] = sc.inc("tensor", ST)
        if block == 1 and scn == 0 and "b1dbg" in b.cfg.debug:
            d1 = b.dbg("b1dbg_stats", [1, 4 * SC], F32)
            sc.wait("sync", SS)
            sc.wait("sync", SV)
            sc.dma("sync", d1[:, 0:SC], mu, sem=b.ch("dbg1"))
            sc.dma("sync", d1[:, SC:2 * SC], rstd, sem=b.ch("dbg1"))
            sc.dma("sync", d1[:, 2 * SC:3 * SC], var, sem=b.ch("dbg1"))
            d2 = b.dbg("b1dbg_xin", [128, DC, SC], F32)
            d3 = b.dbg("b1dbg_xnb", [128, SC], F32)
            sc.dma("sync", d2, xin, sem=b.ch("dbg1"))
            sc.dma("sync", d3, xnB[(DC - 1) % 2], sem=b.ch("dbg1"))
        sc.wait("sync", SS, ss_k[DC - 1])
        sc.dma("sync", b.scr_xn[:, scn * SC:(scn + 1) * SC].rearrange(
            "(k p) s -> p k s", p=128), xnbf, sem=b.ch(f"ln{block}s"))
        if own:
            sc.wait("scalar", ST, st_k[DC - 1])
            if block == 0:
                sc.do("scalar", lambda s, scn=scn: s.copy(
                    b.lg_fm.ap()[0:nlg, scn * SC:(scn + 1) * SC], ps_lg[0:nlg, :]))
            else:
                sc.do("scalar", lambda s, scn=scn: s.activation(
                    b.lg_fm.ap()[0:nlg, scn * SC:(scn + 1) * SC], ps_lg[0:nlg, :],
                    AF.Identity, bias=b.consts["gbsum"].ap()))
            lgss = sc.inc("scalar", SS)
            sc.wait("tensor", SS, lgss)
    b.run_phase(name, sc, [("xn", b.scr_xn), ("lg", b.lg_fm.ap()[0:nlg, :])])


def p_fin(b):
    sc = b.sched()
    b.run_phase("fin", sc, final=True)


def per_core_inputs(x, shared, cfg, core):
    bseq, half = core // 2, core % 2
    T, S = cfg.T, cfg.S
    xs = x[bseq].T
    if half == 0:
        xrot = xs
        pos = np.arange(S, dtype=np.float32)
    else:
        xrot = np.concatenate([xs[:, T:], xs[:, :T]], axis=1)
        pos = np.concatenate([np.arange(T, S), np.arange(0, T)]).astype(np.float32)
    d = dict(shared)
    d["xseq"] = np.ascontiguousarray(xrot.astype(np.float32))
    d["pmask"] = np.full((128, 1), NEG if half == 0 else 0.0, dtype=np.float32)
    d["pos_rot"] = pos.reshape(1, S)
    return d


def build_program(cfg, mode="fused", group=None, upto=None):
    b = Builder(cfg, mode, group)
    phases = []
    if mode == "fused" or group == 0:
        phases += [("consts", lambda: p_consts(b)),
                   ("b0_ln", lambda: p_ln_outer(b, 0)),
                   ("b02", lambda: p_b02(b)),
                   ("b03", lambda: p_b03(b)),
                   ("b0_attn", lambda: p_attn(b, 0)),
                   ("b05", lambda: p_b05(b)),
                   ("b067", lambda: p_b067(b)),
                   ("b08", lambda: p_b08(b))]
    if mode == "fused":
        phases += [("cc", lambda: p_cc(b))]
    if mode == "fused" or group == 1:
        phases += [("b1_ln", lambda: p_ln_outer(b, 1)),
                   ("b12", lambda: p_b12(b)),
                   ("b13", lambda: p_b13(b)),
                   ("rope", lambda: p_rope(b)),
                   ("lxn", lambda: p_loadxn(b)),
                   ("b1_attn", lambda: p_attn(b, 1)),
                   ("b16", lambda: p_b16(b))]
    for nm, fn in phases:
        fn()
        if upto == nm:
            break
    p_fin(b)
    return b


def p_b02(b):
    cfg = b.cfg
    T, W = cfg.T, cfg.SC
    NT = T // W
    V = b.view
    o = 0
    lgt = V(o, [128, 3], F32); o += 3 * 4
    m = V(o, [128, 1], F32); o += 4
    ge0 = V(o, [128, 1], F32); o += 4
    ge1 = V(o, [128, 1], F32); o += 4
    nge0 = V(o, [128, 1], F32); o += 4
    nge1 = V(o, [128, 1], F32); o += 4
    sel12 = V(o, [128, 2], F32); o += 2 * 4
    o = 64
    sel1fm = V(o, [1, T], F32); o += T * 4
    sel2fm = V(o, [1, T], F32); o += T * 4
    sel1f = V(o, [128, W], U8); o += W * 2
    t1bf = V(o, [32, W], BF16); o += W * 2
    xnc = V(o, [128, DC, W], BF16); o += DC * W * 2
    o0 = V(o, [128, W], F32); o += W * 4
    o1 = V(o, [128, W], F32); o += W * 4
    g01 = [V(o, [128, W], F32), V(o + W * 4, [128, W], F32)]; o += 2 * W * 4
    ps_t, ps_sel, ps_t1, ps_lr, ps_b = (p.ap() for p in b.ps[:5])
    HW, ST, SV, SS = b.SEM["HW"], b.SEM["T"], b.SEM["V"], b.SEM["S"]
    C = b.consts
    sc = b.sched()
    for tt in range(T // 128):
        cols = slice(tt * 128, (tt + 1) * 128)
        sc.wait("tensor", SV)
        sc.wait("tensor", SS)
        sc.do("tensor", lambda t, cols=cols: t.transpose(
            ps_t[:, 0:3], b.lg_fm.ap()[0:3, cols], C["ident_f32"].ap()[0:3, 0:3]))
        sc.inc("tensor", ST)
        sc.wait("vector", ST)
        sc.do("vector", lambda v: v.tensor_copy(lgt, ps_t[:, 0:3]))
        sc.drain("vector")
        sc.do("vector", lambda v: v.reduce_max(m, lgt, axis=AX))
        sc.drain("vector")
        sc.do("vector", lambda v: v.tensor_scalar(ge0, lgt[:, 0:1], m, None, ALU.is_ge))
        sc.do("vector", lambda v: v.tensor_scalar(ge1, lgt[:, 1:2], m, None, ALU.is_ge))
        sc.drain("vector")
        sc.do("vector", lambda v: v.tensor_scalar(nge0, ge0, -1.0, 1.0, ALU.mult, ALU.add))
        sc.do("vector", lambda v: v.tensor_scalar(nge1, ge1, -1.0, 1.0, ALU.mult, ALU.add))
        sc.drain("vector")
        sc.do("vector", lambda v: v.tensor_mul(sel12[:, 0:1], nge0, ge1))
        sc.do("vector", lambda v: v.tensor_mul(sel12[:, 1:2], nge0, nge1))
        sc.inc("vector", SV)
        sc.wait("tensor", SV)
        sc.do("tensor", lambda t: t.transpose(ps_sel[0:1, 0:128], sel12[:, 0:1],
                                              C["ident_f32"].ap()))
        sc.do("tensor", lambda t: t.transpose(ps_sel[0:1, 128:256], sel12[:, 1:2],
                                              C["ident_f32"].ap()))
        sc.inc("tensor", ST)
        sc.wait("scalar", ST)
        sc.do("scalar", lambda s, cols=cols: s.copy(sel1fm[:, cols], ps_sel[0:1, 0:128]))
        sc.do("scalar", lambda s, cols=cols: s.copy(sel2fm[:, cols], ps_sel[0:1, 128:256]))
        sc.inc("scalar", SS)
    for tc in range(NT):
        cols = slice(tc * W, (tc + 1) * W)
        sc.wait("tensor", SS)
        sc.do("tensor", lambda t, cols=cols: t.matmul(
            ps_b[:, 0:W], C["ones_row_f"].ap(), sel2fm[:, cols], start=True, stop=True))
        sc.inc("tensor", ST)
        sc.wait("scalar", ST)
        sc.do("scalar", lambda s, cols=cols: s.copy(b.sel2f.ap()[:, cols], ps_b[:, 0:W]))
        sc.inc("scalar", SS)
        sc.wait("tensor", SS)
    for tc in range(NT):
        cols = slice(tc * W, (tc + 1) * W)
        sc.do("tensor", lambda t, cols=cols: t.matmul(
            ps_b[:, 0:W], C["ones_row_f"].ap(), sel1fm[:, cols], start=True, stop=True))
        sc.inc("tensor", ST)
        sc.wait("scalar", ST)
        sc.do("scalar", lambda s: s.copy(sel1f, ps_b[:, 0:W]))
        sc.inc("scalar", SS)
        sc.wait("sync", SV)
        sc.wait("sync", SS)
        sc.dma("sync", xnc, b.scr_xn[:, cols].rearrange("(k p) s -> p k s", p=128))
        ld = HW.n
        sc.wait("tensor", HW, ld)
        for k in range(DC):
            sc.do("tensor", lambda t, k=k: t.matmul(
                ps_t1[0:32, 0:W], C["wdown_til"].ap()[:, k, :], xnc[:, k, :],
                start=(k == 0), stop=(k == DC - 1)))
        sc.inc("tensor", ST)
        sc.wait("scalar", ST)
        sc.do("scalar", lambda s: s.copy(t1bf, ps_t1[0:32, 0:W]))
        sc.inc("scalar", SS)
        for k in range(DC):
            sc.wait("tensor", SS)
            sc.wait("tensor", SV)
            sc.do("tensor", lambda t, k=k: t.matmul(
                ps_lr[:, 0:W], C["wup_T"].ap()[0:32, k * 128:(k + 1) * 128], t1bf,
                start=True, stop=True))
            stk = sc.inc("tensor", ST)
            sc.wait("scalar", HW, ld)
            if k >= 1:
                sc.wait("scalar", SV)
            sc.do("scalar", lambda s, k=k: s.activation(
                o0, xnc[:, k, :], AF.Identity, bias=C["sbias_T"].ap()[:, k:k + 1]))
            ssk = sc.inc("scalar", SS)
            sc.wait("vector", ST, stk)
            sc.drain("vector")
            sc.do("vector", lambda v, k=k: v.tensor_add(o1, ps_lr[:, 0:W], xnc[:, k, :]))
            sc.inc("vector", SV)
            sc.wait("vector", SS, ssk)
            if k >= 2:
                sc.wait("vector", b.ch(f"b02g{k % 2}"))
            sc.drain("vector")
            sc.do("vector", lambda v, k=k: v.select(g01[k % 2], sel1f, o1, o0,
                                                    add_drain=True))
            sc.inc("vector", SV)
            sc.wait("sync", SV)
            sc.dma("sync", b.scr_g[k * 128:(k + 1) * 128, cols], g01[k % 2],
                   sem=b.ch(f"b02g{k % 2}"))
    b.run_phase("b02", sc, [("g", b.scr_g), ("sel2f", b.sel2f.ap())])


def xn1_region(b):
    return b.view(0, [128, DC, b.cfg.S], BF16)


def p_b03(b):
    cfg = b.cfg
    S, SC = cfg.S, cfg.SC
    NSC = S // SC
    V = b.view
    XN1 = xn1_region(b)
    o = DC * S * 2
    xnc = V(o, [128, DC, SC], BF16); o += DC * SC * 2
    sqbf = V(o, [128, DC, SC], BF16); o += DC * SC * 2
    xnA = [V(o, [128, SC], F32), V(o + SC * 4, [128, SC], F32)]; o += 2 * SC * 4
    xnB = [V(o, [128, SC], F32), V(o + SC * 4, [128, SC], F32)]; o += 2 * SC * 4
    mu = V(o, [1, SC], F32); o += SC * 4
    musq = V(o, [1, SC], F32); o += SC * 4
    var = V(o, [1, SC], F32); o += SC * 4
    lnv = V(o, [1, SC], F32); o += SC * 4
    rstd = V(o, [1, SC], F32); o += SC * 4
    ps_sum, ps_sq, ps_mu, ps_rstd = (p.ap()[:, 0:SC] for p in b.ps[:4])
    HW, ST, SV, SS = b.SEM["HW"], b.SEM["T"], b.SEM["V"], b.SEM["S"]
    C = b.consts
    sc = b.sched()
    for scn in range(NSC):
        cols = slice(scn * SC, (scn + 1) * SC)
        sc.wait("sync", SV)
        sc.wait("sync", SS)
        sc.dma("sync", xnc, b.scr_xn[:, cols].rearrange("(k p) s -> p k s", p=128))
        ld = HW.n
        sc.wait("scalar", HW, ld)
        for k in range(DC):
            sc.do("scalar", lambda s, k=k: s.activation(sqbf[:, k, :], xnc[:, k, :],
                                                        AF.Square))
            sc.inc("scalar", SS)
        for k in range(DC):
            sc.wait("tensor", SS, SS.n - (DC - 1 - k))
            if k == 0:
                sc.wait("tensor", HW, ld)
            sc.do("tensor", lambda t, k=k: t.matmul(
                ps_sum[0:1, :], C["ones_col_bf"].ap(), xnc[:, k, :],
                start=(k == 0), stop=(k == DC - 1)))
            sc.do("tensor", lambda t, k=k: t.matmul(
                ps_sq[0:1, :], C["ones_col_bf"].ap(), sqbf[:, k, :],
                start=(k == 0), stop=(k == DC - 1)))
        sc.inc("tensor", ST)
        sc.wait("scalar", ST)
        sc.do("scalar", lambda s: s.activation(mu, ps_sum[0:1, :], AF.Copy, scale=1.0 / D))
        sc.inc("scalar", SS)
        sc.wait("vector", SS)
        sc.do("vector", lambda v: v.tensor_mul(musq, mu, mu))
        sc.drain("vector")
        sc.do("vector", lambda v: v.tensor_scalar_sub(musq, musq, 1e-5))
        sc.drain("vector")
        sc.do("vector", lambda v: v.scalar_tensor_tensor(
            var, ps_sq[0:1, :], 1.0 / D, musq, ALU.mult, ALU.subtract))
        sc.inc("vector", SV)
        sc.wait("scalar", SV)
        sc.do("scalar", lambda s: s.activation(lnv, var, AF.Ln))
        sc.drain("scalar")
        sc.do("scalar", lambda s: s.activation(rstd, lnv, AF.Exp, scale=-0.5))
        sc.inc("scalar", SS)
        sc.wait("tensor", SS)
        sc.do("tensor", lambda t: t.matmul(ps_mu, C["ones_row_f"].ap(), mu,
                                           start=True, stop=True))
        sc.do("tensor", lambda t: t.matmul(ps_rstd, C["ones_row_f"].ap(), rstd,
                                           start=True, stop=True))
        sc.inc("tensor", ST)
        sc.wait("vector", ST)
        ss_k = [0] * DC
        for k in range(DC):
            if k >= 2:
                sc.wait("vector", SS, ss_k[k - 2])
            sc.do("vector", lambda v, k=k: v.tensor_sub(xnA[k % 2], xnc[:, k, :], ps_mu))
            sc.drain("vector")
            sc.do("vector", lambda v, k=k: v.tensor_mul(xnB[k % 2], xnA[k % 2], ps_rstd))
            svk = sc.inc("vector", SV)
            sc.wait("scalar", SV, svk)
            sc.do("scalar", lambda s, k=k, cols=cols: s.activation(
                XN1[:, k, cols], xnB[k % 2], AF.Identity,
                scale=C["ln1_scale"].ap()[:, k:k + 1], bias=C["ln1_bias"].ap()[:, k:k + 1]))
            ss_k[k] = sc.inc("scalar", SS)
    b.run_phase("b03", sc, [("xn1", XN1)])


def rope_region(b):
    off = b.ARN - 2 * b.cfg.S * 4
    return (b.view(off, [64, b.cfg.S], F32),
            b.view(off + b.cfg.S * 4, [64, b.cfg.S], F32))


def p_attn(b, block):
    cfg = b.cfg
    S, T, W = cfg.S, cfg.T, cfg.SC
    KT = S // 128
    NQC = T // W
    NKC = S // W
    POWN = T // 128
    BAND = W // 128
    V = b.view
    XN1 = xn1_region(b)
    o = DC * S * 2
    vgrp = V(o, [128, KT, 512], BF16); o += KT * 512 * 2
    kh = V(o, [128, S], BF16); o += S * 2
    qh = V(o, [128, T], BF16); o += T * 2
    ee = [V(o, [128, W], BF16), V(o + W * 2, [128, W], BF16)]; o += 2 * W * 2
    ssc = V(o, [128, W], F32); o += W * 4
    rr = V(o, [1, W], F32); o += W * 4
    rr2 = V(o, [1, W], F32); o += W * 4
    ctxs = V(o, [128, W], F32); o += W * 4
    qs = [V(o, [128, W], F32), V(o + W * 4, [128, W], F32)]; o += 2 * W * 4
    cosv, sinv = rope_region(b)
    assert o <= b.ARN - (2 * S * 4 if block == 1 else 0)
    ps_v, ps_k, ps_q, ps_s0, ps_s1, ps_ctx, ps_r, ps_b = (p.ap() for p in b.ps)
    HW, ST, SV, SS = b.SEM["HW"], b.SEM["T"], b.SEM["V"], b.SEM["S"]
    C = b.consts
    wv_h = b.hin["l0wv_til" if block == 0 else "l1wv_til"]
    wk_h = b.hin["l0wk_til" if block == 0 else "l1wk_til"]
    wq_h = b.hin["l0wq_til" if block == 0 else "l1wq_til"]
    invsq = 1.0 / math.sqrt(DH)
    sc = b.sched()

    def rope_evac(ps_in, out_bf, cols_pos):
        cosc = cosv[:, cols_pos]
        sinc = sinv[:, cols_pos]
        pA, pB = qs
        sc.do("vector", lambda v, pA=pA: v.tensor_mul(pA[0:64, :],
                                                      ps_in[0:64, :], cosc))
        sc.do("vector", lambda v, pB=pB: v.tensor_mul(pB[0:64, :],
                                                      ps_in[64:128, :], sinc))
        sc.do("vector", lambda v, pA=pA: v.tensor_mul(pA[64:128, :],
                                                      ps_in[0:64, :], sinc))
        sc.do("vector", lambda v, pB=pB: v.tensor_mul(pB[64:128, :],
                                                      ps_in[64:128, :], cosc))
        sc.drain("vector")
        sc.do("vector", lambda v, pA=pA, pB=pB: v.tensor_sub(
            out_bf[0:64, :], pA[0:64, :], pB[0:64, :]))
        sc.do("vector", lambda v, pA=pA, pB=pB: v.tensor_add(
            out_bf[64:128, :], pA[64:128, :], pB[64:128, :]))
        sc.inc("vector", SV)
        sc.drain("vector")

    vproj_free = [0, 0]
    kq_free = [None, None]
    for g in range(NH // 4):
        heads = list(range(4 * g, 4 * g + 4))
        hv = b.wplan([wv_h[h] for h in heads], quad=True)
        hks, hqs = {}, {}
        for h in heads:
            hks[h] = b.wplan([wk_h[h]])[0]
            hqs[h] = b.wplan([wq_h[h]])[0]
        sc.wloads(hv + [x for h in heads for x in (hks[h], hqs[h])])
        quad = [h for h in hv if not h.null][0]
        for h_ in hv:
            sc.wwait(h_)
        for tt in range(KT):
            bk = [ps_v, ps_k][tt % 2]
            if vproj_free[tt % 2]:
                sc.wait("tensor", SS, vproj_free[tt % 2])
            for k in range(DC):
                sc.do("tensor", lambda t, k=k, tt=tt, quad=quad, bk=bk: t.matmul(
                    bk[:, :], XN1[:, k, tt * 128:(tt + 1) * 128],
                    b.wring[:, quad.slot:quad.slot + 4, k, :],
                    start=(k == 0), stop=(k == DC - 1)))
            ins_i = sc.inc("tensor", ST)
            sc.wait("scalar", ST, ins_i)
            sc.do("scalar", lambda s, tt=tt, bk=bk: s.copy(vgrp[:, tt, :], bk[:, :]))
            if tt == KT - 1:
                for _h in hv:
                    sc.inc("scalar", b.SEM["WF"], 1)
            vproj_free[tt % 2] = sc.inc("scalar", SS)
        kq_free[0] = (SS.h, vproj_free[1])
        for hh, h in enumerate(heads):
            sc.wwait(hks[h])
            for sct in range(NKC):
                cols = slice(sct * W, (sct + 1) * W)
                bk = [ps_k, ps_q][sct % 2]
                if kq_free[sct % 2]:
                    sc.q["tensor"].append(("wait",) + kq_free[sct % 2])
                for k in range(DC):
                    sc.do("tensor", lambda t, k=k, hk=hks[h], cols=cols, bk=bk: t.matmul(
                        bk[:, 0:W], b.wview(hk, k), XN1[:, k, cols],
                        start=(k == 0), stop=(k == DC - 1)))
                kst = sc.inc("tensor", ST)
                if block == 0:
                    sc.wait("scalar", ST, kst)
                    sc.do("scalar", lambda s, cols=cols, bk=bk: s.copy(kh[:, cols],
                                                                       bk[:, 0:W]))
                    if sct == NKC - 1:
                        sc.inc("scalar", b.SEM["WF"], 1)
                    kq_free[sct % 2] = (SS.h, sc.inc("scalar", SS))
                else:
                    sc.wait("vector", ST, kst)
                    rope_evac(bk[:, 0:W], kh[:, cols], cols)
                    if sct == NKC - 1:
                        sc.inc("vector", b.SEM["WF"], 1)
                    kq_free[sct % 2] = (SV.h, SV.n)
            sc.wwait(hqs[h])
            for qct in range(NQC):
                cols = slice(qct * W, (qct + 1) * W)
                bk = [ps_k, ps_q][qct % 2]
                if kq_free[qct % 2]:
                    sc.q["tensor"].append(("wait",) + kq_free[qct % 2])
                for k in range(DC):
                    sc.do("tensor", lambda t, k=k, hq=hqs[h], cols=cols, bk=bk: t.matmul(
                        bk[:, 0:W], b.wview(hq, k), XN1[:, k, cols],
                        start=(k == 0), stop=(k == DC - 1)))
                qst = sc.inc("tensor", ST)
                if block == 0:
                    sc.wait("scalar", ST, qst)
                    sc.do("scalar", lambda s, cols=cols, bk=bk: s.copy(qh[:, cols],
                                                                       bk[:, 0:W]))
                    if qct == NQC - 1:
                        sc.inc("scalar", b.SEM["WF"], 1)
                    kq_free[qct % 2] = (SS.h, sc.inc("scalar", SS))
                else:
                    sc.wait("vector", ST, qst)
                    rope_evac(bk[:, 0:W], qh[:, cols], cols)
                    if qct == NQC - 1:
                        sc.inc("vector", b.SEM["WF"], 1)
                    kq_free[qct % 2] = (SV.h, SV.n)
            sc.wait("tensor", SS)
            sc.wait("tensor", SV)
            for qc in range(NQC):
                qcols = slice(qc * W, (qc + 1) * W)
                own_tiles = list(range(0, (qc + 1) * BAND))
                tiles = own_tiles + list(range(POWN, KT))
                n = len(tiles)
                psfree = [None] * n
                expdone = [0] * n
                avdone = [0] * n
                prev_band_exp = [0]

                def emit_av(i):
                    first = (i == 0)
                    last = (i == n - 1)
                    sc.wait("tensor", SS, expdone[i])
                    kt = tiles[i]
                    sc.do("tensor", lambda t, kt=kt, hh=hh, i=i, first=first, last=last:
                          t.matmul(ps_ctx[:, 0:W],
                                   vgrp[:, kt, hh * 128:(hh + 1) * 128], ee[i % 2],
                                   start=first, stop=last))
                    sc.do("tensor", lambda t, i=i, first=first, last=last: t.matmul(
                        ps_r[0:1, 0:W], C["ones_col_bf"].ap(), ee[i % 2],
                        start=first, stop=last))
                    avdone[i] = sc.inc("tensor", ST)

                for i, kt in enumerate(tiles):
                    ps_s = ps_s0 if i % 2 == 0 else ps_s1
                    if i >= 2:
                        sc.q["tensor"].append(("wait",) + psfree[i - 2])
                    sc.do("tensor", lambda t, kt=kt, ps_s=ps_s, qcols=qcols: t.matmul(
                        ps_s[:, 0:W], kh[:, kt * 128:(kt + 1) * 128], qh[:, qcols],
                        start=True, stop=True))
                    smst = sc.inc("tensor", ST)
                    bi = kt - qc * BAND
                    if i >= 2:
                        e_guard = [("tensor", ST, avdone[i - 2])]
                    else:
                        e_guard = []
                    if kt in own_tiles and 0 <= bi < BAND:
                        sc.wait("vector", ST, smst)
                        if prev_band_exp[0]:
                            sc.wait("vector", SS, prev_band_exp[0])
                        sc.drain("vector")
                        sc.do("vector", lambda v, ps_s=ps_s, bi=bi: v.tensor_add(
                            ssc, ps_s[:, 0:W], C["maskband"].ap()[:, bi, :]))
                        bsv = sc.inc("vector", SV)
                        psfree[i] = (SS.h, 0)
                        psfree[i] = (SV.h, bsv)
                        sc.wait("scalar", SV, bsv)
                        for gname, gh, gv in e_guard:
                            sc.wait("scalar", b.SEM["T"], gv)
                        sc.do("scalar", lambda s, eidx=i % 2: s.activation(
                            ee[eidx], ssc, AF.Exp, scale=invsq, bias=C["cshift"].ap()))
                        expdone[i] = sc.inc("scalar", SS)
                        prev_band_exp[0] = expdone[i]
                    else:
                        bias_ap = C["cshift"] if kt in own_tiles else C["pm_c"]
                        sc.wait("scalar", ST, smst)
                        for gname, gh, gv in e_guard:
                            sc.wait("scalar", b.SEM["T"], gv)
                        sc.do("scalar", lambda s, eidx=i % 2, ps_s=ps_s, bias_ap=bias_ap:
                              s.activation(ee[eidx], ps_s[:, 0:W], AF.Exp,
                                           scale=invsq, bias=bias_ap.ap()))
                        expdone[i] = sc.inc("scalar", SS)
                        psfree[i] = (SS.h, expdone[i])
                    if i >= 1:
                        emit_av(i - 1)
                emit_av(n - 1)
                sc.wait("scalar", ST)
                sc.do("scalar", lambda s: s.activation(rr2, ps_r[0:1, 0:W], AF.Ln))
                sc.drain("scalar")
                sc.do("scalar", lambda s: s.activation(rr, rr2, AF.Exp, scale=-1.0))
                rcd = sc.inc("scalar", SS)
                sc.wait("tensor", SS, rcd)
                sc.do("tensor", lambda t: t.matmul(ps_b[:, 0:W], C["ones_row_f"].ap(),
                                                   rr, start=True, stop=True))
                bst = sc.inc("tensor", ST)
                sc.wait("scalar", ST, bst)
                sc.do("scalar", lambda s: s.copy(ctxs, ps_ctx[:, 0:W]))
                csk = sc.inc("scalar", SS)
                sc.wait("vector", SS, csk)
                sc.do("vector", lambda v, h=h, qcols=qcols: v.tensor_mul(
                    b.ctx_all[:, h, qcols], ctxs, ps_b[:, 0:W]))
                sc.inc("vector", SV)
                sc.wait("tensor", SV)
    b.run_phase(f"b{block}_attn", sc, [("ctx", b.ctx_all.ap())])


def p_b05(b):
    cfg = b.cfg
    T, W = cfg.T, cfg.SC
    NT = T // W
    V = b.view
    o = 0
    xnc = V(o, [128, DC, T], BF16); o += DC * T * 2
    ybuf = [V(o, [128, W], F32), V(o + W * 4, [128, W], F32)]; o += 2 * W * 4
    ps_y = [b.ps[0].ap(), b.ps[1].ap()]
    HW, ST, SV, SS = (b.SEM[x] for x in ("HW", "T", "V", "S"))
    sc = b.sched()
    hw_chunks = b.wplan([b.hin["l0wo_til"][k] for k in range(DC)])
    sc.wloads(hw_chunks)
    chx = b.ch("b05x")
    for tc in range(NT):
        cols = slice(tc * W, (tc + 1) * W)
        sc.dma("sync", xnc[:, :, cols],
               b.scr_xn[:, cols].rearrange("(k p) s -> p k s", p=128), sem=chx)
    sc.wait("tensor", chx)
    sc.wait("vector", chx)
    svslot = [0, 0]
    for k in range(DC):
        sc.wwait(hw_chunks[k])
        for tc in range(NT):
            cols = slice(tc * W, (tc + 1) * W)
            i2 = (k * NT + tc) % 2
            if svslot[i2]:
                sc.wait("tensor", SV, svslot[i2])
            for hc in range(DC):
                sc.do("tensor", lambda t, k=k, hc=hc, cols=cols, i2=i2: t.matmul(
                    ps_y[i2][:, 0:W], b.wview(hw_chunks[k], hc),
                    b.ctx_all[:, hc, cols], start=(hc == 0), stop=(hc == DC - 1)))
            mmk = sc.inc("tensor", ST)
            sc.wait("vector", ST, mmk)
            sc.wait("vector", b.ch(f"b05y{i2}"))
            sc.drain("vector")
            sc.do("vector", lambda v, k=k, i2=i2, cols=cols: v.tensor_add(
                ybuf[i2], ps_y[i2][:, 0:W], xnc[:, k, cols]))
            if tc == NT - 1:
                sc.inc("vector", b.SEM["WF"], 1)
            svslot[i2] = sc.inc("vector", SV)
            sc.wait("sync", SV, svslot[i2])
            sc.dma("sync", b.scr_y[k * 128:(k + 1) * 128, cols], ybuf[i2],
                   sem=b.ch(f"b05y{i2}"))
    b.run_phase("b05", sc, [("y", b.scr_y)])


def p_b067(b, stages=3):
    cfg = b.cfg
    T, W = cfg.T, cfg.SC
    NT = T // W
    V = b.view
    o = 0
    yn2 = V(o, [128, DC, T], BF16); o += DC * T * 2
    h1h = V(o, [128, 32, T], BF16); o += 32 * T * 2
    yk = [V(o, [128, W], F32), V(o + W * 4, [128, W], F32)]; o += 2 * W * 4
    oB = o
    ybf = [V(oB, [128, W], BF16), V(oB + W * 2, [128, W], BF16)]
    sqbf = [V(oB + 2 * W * 2, [128, W], BF16), V(oB + 3 * W * 2, [128, W], BF16)]
    xnA = V(oB + 4 * W * 2, [128, W], F32)
    xnB = V(oB + 4 * W * 2 + W * 4, [128, W], F32)
    o2 = oB + 4 * W * 2 + 2 * W * 4
    mu = V(o2, [1, W], F32)
    musq = V(o2 + W * 4, [1, W], F32)
    var = V(o2 + 2 * W * 4, [1, W], F32)
    lnv = V(o2 + W * 4, [1, W], F32)
    rstd = V(o2 + 2 * W * 4, [1, W], F32)
    gold = [V(oB, [128, W], F32), V(oB + W * 4, [128, W], F32)]
    gsel = [V(oB + 2 * W * 4, [128, W], F32), V(oB + 3 * W * 4, [128, W], F32)]
    fpart = [V(oB + 4 * W * 4, [128, W], F32), V(oB + 5 * W * 4, [128, W], F32)]
    o2b = V(oB + 6 * W * 4, [128, W], F32)
    endB = oB + max(4 * W * 2 + 2 * W * 4 + 3 * W * 4, 7 * W * 4)
    assert endB <= b.ARN, endB
    ps_sum, ps_sq, ps_mu, ps_rstd = (p.ap()[:, 0:W] for p in b.ps[:4])
    ps_f = [b.ps[4].ap(), b.ps[5].ap()]
    ps_o = [b.ps[6].ap(), b.ps[7].ap()]
    HW, HWS, ST, SV, SS = (b.SEM[x] for x in ("HW", "HWS", "T", "V", "S"))
    C = b.consts
    sc = b.sched()
    ff1_h = b.hin["ff1_til"]
    ff2_h = b.hin["ff2_til"]
    for tc in range(NT):
        cols = slice(tc * W, (tc + 1) * W)
        ssl = [0, 0]
        stk = [0, 0]
        for k in range(DC):
            sem_k = b.SEM["HA"] if k % 2 == 0 else b.SEM["HB"]
            if k >= 2:
                sc.wait("sync", SS, ssl[k % 2])
            sc.dma("sync", yk[k % 2], b.scr_y[k * 128:(k + 1) * 128, cols], sem=sem_k)
            ldk = sem_k.n
            sc.wait("scalar", sem_k, ldk)
            if k >= 2:
                sc.wait("scalar", ST, stk[k % 2])
            sc.do("scalar", lambda s, k=k: s.copy(ybf[k % 2], yk[k % 2]))
            sc.drain("scalar")
            sc.do("scalar", lambda s, k=k: s.activation(sqbf[k % 2], yk[k % 2], AF.Square))
            ssl[k % 2] = sc.inc("scalar", SS)
            sc.wait("tensor", SS, ssl[k % 2])
            sc.do("tensor", lambda t, k=k: t.matmul(
                ps_sum[0:1, :], C["ones_col_bf"].ap(), ybf[k % 2],
                start=(k == 0), stop=(k == DC - 1)))
            sc.do("tensor", lambda t, k=k: t.matmul(
                ps_sq[0:1, :], C["ones_col_bf"].ap(), sqbf[k % 2],
                start=(k == 0), stop=(k == DC - 1)))
            stk[k % 2] = sc.inc("tensor", ST)
        sc.wait("scalar", ST)
        sc.do("scalar", lambda s: s.activation(mu, ps_sum[0:1, :], AF.Copy, scale=1.0 / D))
        sc.inc("scalar", SS)
        sc.wait("vector", SS)
        sc.do("vector", lambda v: v.tensor_mul(musq, mu, mu))
        sc.drain("vector")
        sc.do("vector", lambda v: v.tensor_scalar_sub(musq, musq, 1e-5))
        sc.drain("vector")
        sc.do("vector", lambda v: v.scalar_tensor_tensor(
            var, ps_sq[0:1, :], 1.0 / D, musq, ALU.mult, ALU.subtract))
        sc.inc("vector", SV)
        sc.wait("scalar", SV)
        sc.do("scalar", lambda s: s.activation(lnv, var, AF.Ln))
        sc.drain("scalar")
        sc.do("scalar", lambda s: s.activation(rstd, lnv, AF.Exp, scale=-0.5))
        sc.inc("scalar", SS)
        sc.wait("tensor", SS)
        sc.do("tensor", lambda t: t.matmul(ps_mu, C["ones_row_f"].ap(), mu,
                                           start=True, stop=True))
        sc.do("tensor", lambda t: t.matmul(ps_rstd, C["ones_row_f"].ap(), rstd,
                                           start=True, stop=True))
        sc.inc("tensor", ST)
        sc.wait("vector", ST)
        for k in range(DC):
            sc.wait("sync", SV)
            cha = b.ch(f"b067a{k % 2}")
            sc.dma("sync", yk[k % 2], b.scr_y[k * 128:(k + 1) * 128, cols], sem=cha)
            sc.wait("vector", cha)
            sc.wait("vector", SS)
            sc.drain("vector")
            sc.do("vector", lambda v, k=k: v.tensor_sub(xnA, yk[k % 2], ps_mu))
            sc.drain("vector")
            sc.do("vector", lambda v: v.tensor_mul(xnB, xnA, ps_rstd))
            sc.inc("vector", SV)
            sc.wait("scalar", SV)
            sc.do("scalar", lambda s, k=k, cols=cols: s.activation(
                yn2[:, k, cols], xnB, AF.Identity,
                scale=C["ln2_scale"].ap()[:, k:k + 1], bias=C["ln2_bias"].ap()[:, k:k + 1]))
            sc.inc("scalar", SS)
    for half in range(2):
        h_ff1 = b.wplan([ff1_h[32 * half + j] for j in range(32)])
        h_ff2 = {}
        for k in range(DC):
            h_ff2[k] = b.wplan([ff2_h[k][:, 32 * half + 16 * j:32 * half + 16 * (j + 1), :]
                                for j in range(2)])
        sc.wloads(h_ff1 + [x for k in range(DC) for x in h_ff2[k]])
        sc.wait("tensor", SS)
        evff = [0, 0]
        for j in range(32):
            sc.wwait(h_ff1[j])
            for tc in range(NT):
                i2 = (j * NT + tc) % 2
                cols = slice(tc * W, (tc + 1) * W)
                if evff[i2]:
                    sc.wait("tensor", SS, evff[i2])
                for k in range(DC):
                    sc.do("tensor", lambda t, hw=h_ff1[j], k=k, i2=i2, cols=cols: t.matmul(
                        ps_f[i2][:, 0:W], b.wview(hw, k), yn2[:, k, cols],
                        start=(k == 0), stop=(k == DC - 1)))
                fst = sc.inc("tensor", ST)
                sc.wait("scalar", ST, fst)
                sc.do("scalar", lambda s, j=j, i2=i2, cols=cols: s.activation(
                    h1h[:, j, cols], ps_f[i2][:, 0:W], AF.Gelu_apprx_tanh))
                if tc == NT - 1:
                    sc.inc("scalar", b.SEM["WF"], 1)
                evff[i2] = sc.inc("scalar", SS)
        sc.wait("tensor", SS)
        ofree = [None, None]
        for k in range(DC):
            for j in range(2):
                sc.wwait(h_ff2[k][j])
            for tc in range(NT):
                cols = slice(tc * W, (tc + 1) * W)
                i2 = (k * NT + tc) % 2
                if ofree[i2]:
                    sc.q["tensor"].append(("wait",) + ofree[i2])
                for hh in range(32):
                    sc.do("tensor", lambda t, hw=h_ff2[k][hh // 16], hh=hh, i2=i2,
                          cols=cols: t.matmul(
                        ps_o[i2][:, 0:W], b.wview(hw, hh % 16), h1h[:, hh, cols],
                        start=(hh == 0), stop=(hh == 31)))
                okk = sc.inc("tensor", ST)
                chf = b.ch(f"f2f{i2}")
                if half == 0:
                    sc.wait("scalar", ST, okk)
                    sc.wait("scalar", chf)
                    sc.do("scalar", lambda s, i2=i2: s.copy(fpart[i2], ps_o[i2][:, 0:W]))
                    if tc == NT - 1:
                        sc.inc("scalar", b.SEM["WF"], 1)
                        sc.inc("scalar", b.SEM["WF"], 1)
                    ofree[i2] = (SS.h, sc.inc("scalar", SS))
                    sc.wait("sync", SS)
                    sc.dma("sync", b.scr_f[k * 128:(k + 1) * 128, cols], fpart[i2],
                           sem=chf)
                else:
                    chy = b.ch(f"f2y{i2}")
                    chg = b.ch(f"f2g{i2}")
                    sc.wait("sync", SV)
                    sc.wait("sync", chf)
                    sc.dma("sync", yk[i2], b.scr_y[k * 128:(k + 1) * 128, cols], sem=chy)
                    sc.dma("sync", gold[i2], b.scr_g[k * 128:(k + 1) * 128, cols], sem=chg)
                    sc.dma("sync", fpart[i2], b.scr_f[k * 128:(k + 1) * 128, cols],
                           sem=chf)
                    sc.wait("vector", ST, okk)
                    sc.wait("vector", chy)
                    sc.wait("vector", chg)
                    sc.wait("vector", chf)
                    sc.wait("vector", b.ch(f"f2s{i2}"))
                    sc.drain("vector")
                    sc.do("vector", lambda v, i2=i2: v.tensor_add(o2b, ps_o[i2][:, 0:W],
                                                                  fpart[i2]))
                    sc.drain("vector")
                    sc.do("vector", lambda v, i2=i2: v.tensor_add(gsel[i2], o2b, yk[i2]))
                    sc.drain("vector")
                    sc.do("vector", lambda v, i2=i2, cols=cols: v.select(
                        o2b, b.sel2f.ap()[:, cols], gsel[i2], gold[i2], add_drain=True))
                    sc.drain("vector")
                    sc.do("vector", lambda v, i2=i2: v.tensor_copy(gsel[i2], o2b))
                    if tc == NT - 1:
                        sc.inc("vector", b.SEM["WF"], 1)
                        sc.inc("vector", b.SEM["WF"], 1)
                    ofree[i2] = (SV.h, sc.inc("vector", SV))
                    sc.wait("sync", SV)
                    sc.dma("sync", b.scr_g[k * 128:(k + 1) * 128, cols], gsel[i2],
                           sem=b.ch(f"f2s{i2}"))
    b.run_phase("b067", sc, [("g", b.scr_g)])


def p_b08(b):
    cfg = b.cfg
    T, W = cfg.T, cfg.SC
    NT = T // W
    V = b.view
    o = 0
    xnc = V(o, [128, DC, T], BF16); o += DC * T * 2
    vsum_bf = V(o, [128, T], BF16); o += T * 2
    sbuf_s = V(o, [128, W], F32); o += W * 4
    gbuf = [V(o, [128, W], F32), V(o + W * 4, [128, W], F32)]; o += 2 * W * 4
    gs = V(o, [128, W], F32); o += W * 4
    xo = [V(o, [128, W], F32), V(o + W * 4, [128, W], F32)]; o += 2 * W * 4
    prod = V(o, [128, W], F32); o += W * 4
    xnew = [V(o, [128, W], F32), V(o + W * 4, [128, W], F32)]; o += 2 * W * 4
    assert o <= b.ARN
    ps_v = [b.ps[0].ap(), b.ps[1].ap()]
    ps_s = [b.ps[2].ap(), b.ps[3].ap()]
    HW, HWS, ST, SV, SS = (b.SEM[x] for x in ("HW", "HWS", "T", "V", "S"))
    C = b.consts
    sc = b.sched()
    hv0 = b.wplan([b.hin["wv0_til"][c] for c in range(DC)])
    hwo = b.wplan([b.hin["wout_til"][k] for k in range(DC)])
    sc.wloads(hv0 + hwo)
    chx = b.ch("b08x")
    for tc in range(NT):
        cols = slice(tc * W, (tc + 1) * W)
        sc.dma("sync", xnc[:, :, cols],
               b.scr_xn[:, cols].rearrange("(k p) s -> p k s", p=128), sem=chx)
    sc.wait("tensor", chx)
    for c in range(DC):
        sc.wwait(hv0[c])
        for tc in range(NT):
            for k in range(DC):
                sc.do("tensor", lambda t, c=c, tc=tc, k=k: t.matmul(
                    ps_v[tc % 2][:, 0:W], b.wview(hv0[c], k),
                    xnc[:, k, tc * W:(tc + 1) * W],
                    start=(c == 0 and k == 0), stop=(c == DC - 1 and k == DC - 1)))
        sc.inc("tensor", b.SEM["WF"], 1)
    sc.inc("tensor", ST)
    sc.wait("scalar", ST)
    for tc in range(NT):
        sc.do("scalar", lambda s, tc=tc: s.activation(
            vsum_bf[:, tc * W:(tc + 1) * W], ps_v[tc % 2][:, 0:W], AF.Identity,
            bias=C["bvsum"].ap()))
        sc.inc("scalar", SS)
    sc.wait("tensor", SS)
    for k in range(DC):
        sc.wwait(hwo[k])
        for tc in range(NT):
            cols = slice(tc * W, (tc + 1) * W)
            i2 = (k * NT + tc) % 2
            sc.wait("tensor", SS)
            for hc in range(DC):
                sc.do("tensor", lambda t, k=k, hc=hc, tc=tc, i2=i2: t.matmul(
                    ps_s[i2][:, 0:W], b.wview(hwo[k], hc),
                    vsum_bf[:, tc * W:(tc + 1) * W],
                    start=(hc == 0), stop=(hc == DC - 1)))
            if tc == NT - 1:
                sc.inc("tensor", b.SEM["WF"], 1)
            mmk = sc.inc("tensor", ST)
            sc.wait("sync", SV)
            sc.wait("sync", b.ch(f"b08s{i2}"))
            chg8 = b.ch(f"b08g{i2}")
            cho8 = b.ch(f"b08o{i2}")
            sc.dma("sync", gbuf[i2], b.scr_g[k * 128:(k + 1) * 128, cols], sem=chg8)
            sc.dma("sync", xo[i2], b.hin["xseq"][k * 128:(k + 1) * 128, cols], sem=cho8)
            sc.wait("scalar", ST, mmk)
            sc.wait("scalar", chg8)
            sc.wait("vector", cho8)
            sc.drain("scalar")
            sc.do("scalar", lambda s, i2=i2: s.copy(sbuf_s, ps_s[i2][:, 0:W]))
            sc.drain("scalar")
            sc.do("scalar", lambda s, i2=i2: s.activation(gs, gbuf[i2], AF.Silu))
            sc.inc("scalar", SS)
            sc.wait("vector", SS)
            sc.wait("vector", b.ch(f"b08s{i2}"))
            sc.drain("vector")
            sc.do("vector", lambda v, k=k: v.tensor_scalar_add(
                prod, sbuf_s, C["bout_T"].ap()[:, k:k + 1]))
            sc.drain("vector")
            sc.do("vector", lambda v, i2=i2: v.tensor_mul(xnew[i2], prod, gs))
            sc.drain("vector")
            sc.do("vector", lambda v, i2=i2: v.tensor_add(gbuf[i2], xnew[i2], xo[i2]))
            sc.inc("vector", SV)
            sc.wait("sync", SV)
            sc.dma("sync", b.cc_in.ap()[k * 128:(k + 1) * 128, cols], gbuf[i2],
                   sem=b.ch(f"b08s{i2}"))
    b.run_phase("b08", sc, [("x1", b.cc_in.ap())])


def p_cc(b):
    sc = b.sched()
    groups = [[2 * i, 2 * i + 1] for i in range(b.cfg.ncores // 2)]
    CC = b.SEM["CC"]
    for i2 in range(2):
        sc.wait("gpsimd", b.ch(f"b08s{i2}"))
    sc.do("gpsimd", lambda g: g.collective_compute(
        "AllReduce", ALU.add, replica_groups=groups,
        ins=[b.cc_in.ap()], outs=[b.cc_out.ap()]))
    sc.inc("gpsimd", CC)
    b.run_phase("cc", sc)


def p_b12(b):
    cfg = b.cfg
    T = cfg.T
    BIG = 1.0e30
    V = b.view
    o = 0
    lgt = V(o, [128, 8], F32); o += 8 * 4
    m1 = V(o, [128, 1], F32); o += 4
    m2 = V(o, [128, 1], F32); o += 4
    eq1 = V(o, [128, 8], F32); o += 8 * 4
    eq2 = V(o, [128, 8], F32); o += 8 * 4
    msk = V(o, [128, 8], F32); o += 8 * 4
    dlt = V(o, [128, 1], F32); o += 4
    g2 = V(o, [128, 1], F32); o += 4
    g1 = V(o, [128, 1], F32); o += 4
    t1 = V(o, [128, 8], F32); o += 8 * 4
    comb = V(o, [128, 8], F32); o += 8 * 4
    ps_t, ps_c, ps_c2 = b.ps[0].ap(), b.ps[1].ap(), b.ps[2].ap()
    ST, SV, SS = b.SEM["T"], b.SEM["V"], b.SEM["S"]
    C = b.consts
    sc = b.sched()
    for tt in range(T // 128):
        cols = slice(tt * 128, (tt + 1) * 128)
        sc.wait("tensor", SV)
        sc.wait("tensor", SS)
        sc.do("tensor", lambda t, cols=cols: t.transpose(
            ps_t[:, 0:8], b.lg_fm.ap()[0:8, cols], C["ident_f32"].ap()[0:8, 0:8]))
        sc.inc("tensor", ST)
        sc.wait("vector", ST)
        sc.do("vector", lambda v: v.tensor_copy(lgt, ps_t[:, 0:8]))
        sc.drain("vector")
        sc.do("vector", lambda v: v.reduce_max(m1, lgt, axis=AX))
        sc.drain("vector")
        sc.do("vector", lambda v: v.tensor_scalar(eq1, lgt, m1, None, ALU.is_ge))
        sc.drain("vector")
        sc.do("vector", lambda v: v.scalar_tensor_tensor(msk, eq1, -BIG, lgt,
                                                         ALU.mult, ALU.add))
        sc.drain("vector")
        sc.do("vector", lambda v: v.reduce_max(m2, msk, axis=AX))
        sc.drain("vector")
        sc.do("vector", lambda v: v.tensor_sub(dlt, m2, m1))
        sc.inc("vector", SV)
        sc.wait("scalar", SV)
        sc.do("scalar", lambda s: s.activation(g2, dlt, AF.Sigmoid))
        sc.inc("scalar", SS)
        sc.wait("vector", SS)
        sc.do("vector", lambda v: v.tensor_scalar(g1, g2, -1.0, 1.0, ALU.mult, ALU.add))
        sc.do("vector", lambda v: v.tensor_scalar(eq2, msk, m2, None, ALU.is_ge))
        sc.drain("vector")
        sc.do("vector", lambda v: v.tensor_scalar(t1, eq1, g1, None, ALU.mult))
        sc.drain("vector")
        sc.do("vector", lambda v: v.scalar_tensor_tensor(comb, eq2, g2, t1,
                                                         ALU.mult, ALU.add))
        sc.inc("vector", SV)
        sc.wait("tensor", SV)
        sc.do("tensor", lambda t: t.transpose(ps_c[0:8, 0:128], comb,
                                              C["ident_f32"].ap()))
        sc.inc("tensor", ST)
        sc.wait("scalar", ST)
        sc.do("scalar", lambda s, cols=cols: s.copy(b.comb_fm.ap()[:, cols],
                                                    ps_c[0:8, 0:128]))
        sc.inc("scalar", SS)
    b.run_phase("b12", sc, [("comb", b.comb_fm.ap())])



def p_b13(b):
    cfg = b.cfg
    T, W = cfg.T, cfg.SC
    NT = T // W
    V = b.view
    o = 0
    xnc = V(o, [128, DC, T], BF16); o += DC * T * 2
    hpair = [V(o, [128, HMC, T], BF16), V(o + HMC * T * 2, [128, HMC, T], BF16)]
    o += 2 * HMC * T * 2
    combb = [V(o, [128, T], BF16), V(o + T * 2, [128, T], BF16)]; o += 2 * T * 2
    crow = V(o, [1, W], BF16); o += W * 2
    hs2 = [V(o, [128, W], BF16), V(o + W * 2, [128, W], BF16)]; o += 2 * W * 2
    gbuf = [V(o, [128, W], F32), V(o + W * 4, [128, W], F32)]; o += 2 * W * 4
    gout = [V(o, [128, W], F32), V(o + W * 4, [128, W], F32)]; o += 2 * W * 4
    assert o <= b.ARN, o
    ps_h = [b.ps[0].ap(), b.ps[1].ap()]
    ps_y = [b.ps[2].ap(), b.ps[3].ap()]
    ps_b = b.ps[4].ap()
    HW, ST, SV, SS = (b.SEM[x] for x in ("HW", "T", "V", "S"))
    C = b.consts
    sc = b.sched()
    chx = b.ch("b13x")
    for tc in range(NT):
        cols = slice(tc * W, (tc + 1) * W)
        sc.dma("sync", xnc[:, :, cols],
               b.scr_xn[:, cols].rearrange("(k p) s -> p k s", p=128), sem=chx)
    sc.wait("tensor", chx)
    sc.wait("scalar", chx)
    for p in range(NE // 2):
        es = [2 * p, 2 * p + 1]
        h_w1 = {e: b.wplan([b.hin["w1_til"][e][hc] for hc in range(HMC)]) for e in es}
        h_w2 = {e: [] for e in es}
        for k in range(DC):
            for e in es:
                h_w2[e].append(b.wplan([b.hin["w2_til"][e][k]], cin=HMC)[0])
        sc.wloads(h_w1[es[0]] + h_w1[es[1]] + [h_w2[e][k] for k in range(DC) for e in es])
        for j, e in enumerate(es):
            sc.wait("scalar", SV)
            for tc in range(NT):
                cols = slice(tc * W, (tc + 1) * W)
                sc.wait("tensor", SS)
                sc.do("tensor", lambda t, e=e, cols=cols: t.matmul(
                    ps_b[0:1, 0:W], C["ident_bf"].ap()[0:8, e:e + 1],
                    b.comb_fm.ap()[0:8, cols], start=True, stop=True))
                bst = sc.inc("tensor", ST)
                sc.wait("scalar", ST, bst)
                sc.do("scalar", lambda s: s.copy(crow, ps_b[0:1, 0:W]))
                cst = sc.inc("scalar", SS)
                sc.wait("tensor", SS, cst)
                sc.do("tensor", lambda t: t.matmul(
                    ps_b[:, 0:W], C["ones_row_bf"].ap(), crow, start=True, stop=True))
                bst2 = sc.inc("tensor", ST)
                sc.wait("scalar", ST, bst2)
                sc.do("scalar", lambda s, j=j, cols=cols: s.copy(combb[j][:, cols],
                                                                 ps_b[:, 0:W]))
                sc.inc("scalar", SS)
        hfree = [0, 0]
        hsfree = [0, 0]
        for j, e in enumerate(es):
            for hc in range(HMC):
                sc.wwait(h_w1[e][hc])
                for tc in range(NT):
                    i2 = (hc * NT + tc) % 2
                    if hfree[i2]:
                        sc.wait("tensor", SS, hfree[i2])
                    for k in range(DC):
                        sc.do("tensor", lambda t, hw=h_w1[e][hc], k=k, tc=tc, i2=i2:
                              t.matmul(ps_h[i2][:, 0:W], b.wview(hw, k),
                                       xnc[:, k, tc * W:(tc + 1) * W],
                                       start=(k == 0), stop=(k == DC - 1)))
                    hst = sc.inc("tensor", ST)
                    sc.wait("scalar", ST, hst)
                    if hsfree[i2]:
                        sc.wait("scalar", SV, hsfree[i2])
                    sc.do("scalar", lambda s, e=e, hc=hc, i2=i2: s.activation(
                        hs2[i2], ps_h[i2][:, 0:W], AF.Silu,
                        bias=C["b1_til"].ap()[:, e, hc:hc + 1]))
                    hss = hfree[i2] = sc.inc("scalar", SS)
                    sc.wait("vector", SS, hss)
                    sc.do("vector", lambda v, j=j, hc=hc, tc=tc, i2=i2,
                          cols=slice(tc * W, (tc + 1) * W):
                          v.tensor_mul(hpair[j][:, hc, cols], hs2[i2], combb[j][:, cols]))
                    if tc == NT - 1:
                        sc.inc("vector", b.SEM["WF"], 1)
                    hsfree[i2] = sc.inc("vector", SV)
        sc.wait("tensor", SV)
        yfree = [None, None]
        for k in range(DC):
            for e in es:
                sc.wwait(h_w2[e][k])
            for tc in range(NT):
                cols = slice(tc * W, (tc + 1) * W)
                i2 = (k * NT + tc) % 2
                if yfree[i2]:
                    sc.q["tensor"].append(("wait",) + yfree[i2])
                for j, e in enumerate(es):
                    for hc in range(HMC):
                        last_mm = (j == 1 and hc == HMC - 1 and p > 0)
                        sc.do("tensor", lambda t, hw=h_w2[e][k], hc=hc, j=j, tc=tc,
                              i2=i2, last_mm=last_mm:
                              t.matmul(ps_y[i2][:, 0:W], b.wview(hw, hc),
                                       hpair[j][:, hc, tc * W:(tc + 1) * W],
                                       start=(j == 0 and hc == 0), stop=last_mm))
                if p == 0:
                    sc.do("tensor", lambda t, k=k, cols=cols, i2=i2: t.matmul(
                        ps_y[i2][:, 0:W], C["b2"].ap()[:, k * 128:(k + 1) * 128],
                        b.comb_fm.ap()[:, cols], start=False, stop=True))
                yst = sc.inc("tensor", ST)
                chg = b.ch(f"b13g{i2}")
                if p > 0:
                    sc.wait("sync", SV)
                    sc.dma("sync", gbuf[i2], b.scr_g[k * 128:(k + 1) * 128, cols],
                           sem=chg)
                    sc.wait("vector", chg)
                    sc.wait("vector", ST, yst)
                    sc.wait("vector", b.ch(f"b13s{i2}"))
                    sc.drain("vector")
                    sc.do("vector", lambda v, i2=i2: v.tensor_add(gout[i2],
                                                                  ps_y[i2][:, 0:W],
                                                                  gbuf[i2]))
                    if tc == NT - 1:
                        sc.inc("vector", b.SEM["WF"], 1)
                        sc.inc("vector", b.SEM["WF"], 1)
                    yfree[i2] = (SV.h, sc.inc("vector", SV))
                else:
                    sc.wait("scalar", ST, yst)
                    sc.wait("scalar", b.ch(f"b13s{i2}"))
                    sc.do("scalar", lambda s, i2=i2: s.copy(gout[i2], ps_y[i2][:, 0:W]))
                    if tc == NT - 1:
                        sc.inc("scalar", b.SEM["WF"], 1)
                        sc.inc("scalar", b.SEM["WF"], 1)
                    yfree[i2] = (SS.h, sc.inc("scalar", SS))
                sc.wait("sync", SV if p > 0 else SS)
                sc.dma("sync", b.scr_g[k * 128:(k + 1) * 128, cols], gout[i2],
                       sem=b.ch(f"b13s{i2}"))
    b.run_phase("b13", sc, [("g", b.scr_g)])


def p_rope(b):
    cfg = b.cfg
    S = cfg.S
    NSC = S // 512 if S >= 512 else 1
    CW = min(512, S)
    V = b.view
    cosv, sinv = rope_region(b)
    o = 0
    jidx = V(o, [64, 1], F32); o += 32
    theta = V(o, [64, 1], F32); o += 32
    ang = V(o, [64, CW], F32); o += CW * 4
    ang2 = V(o, [64, CW], F32); o += CW * 4
    kf = V(o, [64, CW], F32); o += CW * 4
    ki = V(o, [64, CW], I32); o += CW * 4
    kb = V(o, [64, CW], F32); o += CW * 4
    red = V(o, [64, CW], F32); o += CW * 4
    cen = V(o, [64, CW], F32); o += CW * 4
    sv = V(o, [64, CW], F32); o += CW * 4
    posr = V(o, [1, S], F32); o += S * 4
    ps_p = b.ps[0].ap()
    ST, SV, SS, SG = (b.SEM[x] for x in ("T", "V", "S", "G"))
    C = b.consts
    PI = math.pi
    sc = b.sched()
    sc.dma("sync", posr, b.hin["pos_rot"], sem=b.ch("rp"))
    sc.q["gpsimd"].append(("wait", b.SEM["BAR"].h, 4 * b.phase_idx))
    sc.do("gpsimd", lambda g: g.iota(jidx, [[0, 1]], channel_multiplier=1,
                                     allow_small_or_imprecise_dtypes=True))
    sc.inc("gpsimd", SG)
    sc.wait("scalar", SG)
    sc.do("scalar", lambda s: s.activation(theta, jidx, AF.Exp,
                                           scale=-math.log(10000.0) / 64.0))
    sc.inc("scalar", SS)

    for scn in range(max(1, S // CW)):
        cols = slice(scn * CW, (scn + 1) * CW)
        sc.wait("tensor", SS)
        sc.wait("tensor", SV)
        if scn == 0:
            sc.wait("tensor", b.ch("rp"))
        sc.do("tensor", lambda t, cols=cols: t.matmul(
            ps_p[0:64, 0:CW], C["ones_row_f"].ap()[:, 0:64],
            posr[:, cols], start=True, stop=True))
        sc.inc("tensor", ST)
        sc.wait("vector", ST)
        sc.wait("vector", SS)
        sc.drain("vector")
        sc.do("vector", lambda v: v.tensor_scalar(ang, ps_p[0:64, 0:CW], theta, None,
                                                  ALU.mult))
        sc.inc("vector", SV)
        for which, dst in (("sin", sinv), ("cos", cosv)):
            a = ang
            if which == "cos":
                sc.drain("vector")
                sc.do("vector", lambda v: v.tensor_scalar_add(ang2, ang, PI / 2.0))
                a = ang2
            sc.drain("vector")
            sc.do("vector", lambda v, a=a: v.tensor_scalar_mul(kf, a, 1.0 / LN2PI))
            sc.drain("vector")
            sc.do("vector", lambda v: v.tensor_copy(ki, kf))
            sc.drain("vector")
            sc.do("vector", lambda v: v.tensor_copy(kb, ki))
            sc.drain("vector")
            sc.do("vector", lambda v, a=a: v.scalar_tensor_tensor(
                red, kb, -LN2PI, a, ALU.mult, ALU.add))
            sc.drain("vector")
            sc.do("vector", lambda v: v.tensor_scalar_add(cen, red, -PI))
            sc.drain("vector")
            sc.do("vector", lambda v: v.tensor_scalar_max(sv, cen, -PI))
            sc.drain("vector")
            sc.do("vector", lambda v: v.tensor_scalar_min(cen, sv, PI))
            sc.inc("vector", SV)
            sc.wait("scalar", SV)
            sc.do("scalar", lambda s: s.activation(sv, cen, AF.Sin))
            sc.inc("scalar", SS)
            sc.wait("vector", SS)
            sc.drain("vector")
            sc.do("vector", lambda v, dst=dst, cols=cols: v.tensor_scalar_mul(
                dst[:, cols], sv, -1.0))
            sc.inc("vector", SV)
    b.run_phase("rope", sc, [("cos", cosv), ("sin", sinv)])


def p_b16(b):
    cfg = b.cfg
    T, W = cfg.T, cfg.SC
    NT = T // W
    V = b.view
    o = DC * cfg.S * 2
    o = 0
    sbuf_s = V(o, [128, W], F32); o += W * 4
    gbuf = [V(o, [128, W], F32), V(o + W * 4, [128, W], F32)]; o += 2 * W * 4
    gs = V(o, [128, W], F32); o += W * 4
    xo = [V(o, [128, W], F32), V(o + W * 4, [128, W], F32)]; o += 2 * W * 4
    prod = V(o, [128, W], F32); o += W * 4
    xnew = [V(o, [128, W], F32), V(o + W * 4, [128, W], F32)]; o += 2 * W * 4
    ps_s = [b.ps[2].ap(), b.ps[3].ap()]
    HW, ST, SV, SS = (b.SEM[x] for x in ("HW", "T", "V", "S"))
    C = b.consts
    sc = b.sched()
    hwo = b.wplan([b.hin["l1wo_til"][k] for k in range(DC)])
    sc.wloads(hwo)
    for k in range(DC):
        sc.wwait(hwo[k])
        for tc in range(NT):
            cols = slice(tc * W, (tc + 1) * W)
            i2 = (k * NT + tc) % 2
            sc.wait("tensor", SS)
            for hc in range(DC):
                sc.do("tensor", lambda t, k=k, hc=hc, tc=tc, i2=i2, cols=cols: t.matmul(
                    ps_s[i2][:, 0:W], b.wview(hwo[k], hc),
                    b.ctx_all[:, hc, cols], start=(hc == 0), stop=(hc == DC - 1)))
            if tc == NT - 1:
                sc.inc("tensor", b.SEM["WF"], 1)
            mmk = sc.inc("tensor", ST)
            sc.wait("sync", SV)
            sc.wait("sync", b.ch(f"b16s{i2}"))
            chg8 = b.ch(f"b16g{i2}")
            cho8 = b.ch(f"b16o{i2}")
            sc.dma("sync", gbuf[i2], b.scr_g[k * 128:(k + 1) * 128, cols], sem=chg8)
            sc.dma("sync", xo[i2], b.cc_in.ap()[k * 128:(k + 1) * 128, cols], sem=cho8)
            sc.wait("scalar", ST, mmk)
            sc.wait("scalar", chg8)
            sc.drain("scalar")
            sc.do("scalar", lambda s, i2=i2: s.copy(sbuf_s, ps_s[i2][:, 0:W]))
            sc.drain("scalar")
            sc.do("scalar", lambda s, i2=i2: s.activation(gs, gbuf[i2], AF.Silu))
            sc.inc("scalar", SS)
            sc.wait("vector", SS)
            sc.wait("vector", cho8)
            sc.wait("vector", b.ch(f"b16s{i2}"))
            sc.drain("vector")
            sc.do("vector", lambda v, i2=i2: v.tensor_mul(xnew[i2], sbuf_s, gs))
            sc.drain("vector")
            sc.do("vector", lambda v, i2=i2: v.tensor_add(gbuf[i2], xnew[i2], xo[i2]))
            sc.inc("vector", SV)
            sc.wait("sync", SV)
            sc.dma("sync", b.y_ext[k * 128:(k + 1) * 128, cols], gbuf[i2],
                   sem=b.ch(f"b16s{i2}"))
    b.run_phase("b16", sc, [])


def p_loadxn(b):
    cfg = b.cfg
    S, SC = cfg.S, cfg.SC
    XN1 = xn1_region(b)
    sc = b.sched()
    chx = b.ch("lxn")
    for scn in range(S // SC):
        cols = slice(scn * SC, (scn + 1) * SC)
        sc.dma("sync", XN1[:, :, cols],
               b.scr_xn[:, cols].rearrange("(k p) s -> p k s", p=128), sem=chx)
    sc.wait("sync", chx)
    b.run_phase("lxn", sc)



_BUILD_CACHE = {}


def _get_program(mode, group=None):
    key = (mode, group)
    if key not in _BUILD_CACHE:
        cfg = Cfg(ncores=8, S=2048)
        _BUILD_CACHE[key] = (cfg, build_program(cfg, mode=mode, group=group))
    return _BUILD_CACHE[key]


def kernel(**inputs):
    cfg, b = _get_program("fused")
    shared = host_layout(inputs, cfg)
    x = np.asarray(inputs["L0_attn_x"], dtype=np.float32)
    declared = set(b.hin.keys())
    in_maps = []
    for c in range(cfg.ncores):
        m = per_core_inputs(x, shared, cfg, c)
        in_maps.append({k: v for k, v in m.items() if k in declared})
    res = run_bass_kernel_spmd(b.nc, in_maps, core_ids=list(range(cfg.ncores)))
    out = np.empty((4, 2048, 2048), dtype=np.float32)
    T = cfg.T
    for c in range(cfg.ncores):
        bq, half = c // 2, c % 2
        out[bq, half * T:(half + 1) * T, :] = res.results[c]["y_ext"].T
    return out
